# version 23
# baseline (speedup 1.0000x reference)
import contextlib
import numpy as np
import ml_dtypes
import concourse.bass as bass
import concourse.mybir as mybir
from concourse.bass_utils import run_bass_kernel_spmd

F32 = mybir.dt.float32
BF16 = mybir.dt.bfloat16
U32 = mybir.dt.uint32
I32 = mybir.dt.int32
AF = mybir.ActivationFunctionType
ALU = mybir.AluOpType
AX = mybir.AxisListType

N_CORES = 8

COMPUTE_OWNER = {"pe": "tensor", "act": "scalar", "dve": "vector"}
DMA_SEMS = ("dc", "dx0", "dx1", "dx2", "dx3", "dw0", "dw1", "dw2", "dn0",
            "dn1", "dp0", "dp1", "dg0", "dg1", "dg2", "dg3", "dwt0",
            "dwt1", "dwto", "dst", "dsc", "dxn")
ALL_SEMS = ["pe", "act", "dve", "pool"] + list(DMA_SEMS)


class Sched:
    ENGINES = ("tensor", "vector", "scalar", "gpsimd", "sync")

    def __init__(self):
        self.ops = []
        self.counts = {s: 0 for s in ALL_SEMS}
        self.emitted_upto = 0
        self.pending_barrier = {}

    def add(self, engine, fn, waits=(), inc=None, amt=1):
        ws = []
        for w in waits:
            if w is None:
                continue
            if len(w) == 2:
                s, v = w
                force = False
            else:
                s, v, force = w
            if s in DMA_SEMS:
                v = self.counts[s]
            if v <= 0:
                continue
            ws.append((s, v, force))
        if self.pending_barrier.get(engine):
            ws.extend(self.pending_barrier.pop(engine))
        tok = None
        if inc is not None:
            self.counts[inc] += amt
            tok = (inc, self.counts[inc])
        self.ops.append((engine, fn, tuple(ws), inc, amt))
        return tok

    def dma(self, queue_engine, sem, out, in_, waits=()):
        def fn(eng, out=out, in_=in_):
            return eng.dma_start(out=out, in_=in_)
        return self.add(queue_engine, fn, waits=waits, inc=sem, amt=16)

    def barrier(self, exclude=()):
        snap = [(s, v, False) for s, v in self.counts.items()
                if v > 0 and s not in exclude]
        for e in self.ENGINES:
            self.pending_barrier[e] = list(snap)

    def emit(self, block, sems):
        ops = self.ops[self.emitted_upto:]
        self.emitted_upto = len(self.ops)
        for eng_name in self.ENGINES:
            eops = [(fn, ws, inc, amt) for (e, fn, ws, inc, amt) in ops
                    if e == eng_name]
            if not eops:
                continue

            def body(eng, eops=eops, eng_name=eng_name):
                hw = {}
                for (fn, ws, inc, amt) in eops:
                    for (s, v, force) in ws:
                        if hw.get(s, 0) >= v:
                            continue
                        eng.wait_ge(sems[s], v)
                        hw[s] = v
                    inst = fn(eng)
                    if inc is not None:
                        inst.then_inc(sems[inc], amt)
            getattr(block, eng_name)(body)


class Buf:
    def __init__(self):
        self.w = None
        self.readers = []

    def write_waits(self):
        ws = list(self.readers)
        if self.w is not None:
            ws.append(self.w)
        return ws

    def wrote(self, tok):
        self.w = tok
        self.readers = []

    def read(self, tok):
        if tok is not None:
            self.readers.append(tok)


FULL_CFG = dict(B=2, T=2048, CCH=8, NE=8, HJ=32, GD=16)


def build_program(cfg, nc):
    B = cfg["B"]
    T = cfg["T"]
    CCH = cfg["CCH"]
    C = CCH * 128
    HC = CCH
    NE = cfg["NE"]
    HJ = cfg["HJ"]
    HID = HJ * 128
    GD = cfg["GD"]
    TOK = B * T
    NTT = TOK // 128
    QT = min(512, T)
    NQT = T // QT
    KC = T // 128
    G = 256
    NG = (TOK + NE * (G - 1) + G - 1) // G
    NSLOT = NG * G

    x_d = nc.dram_tensor("x", [TOK, C], F32, kind="ExternalInput").ap()
    w_src = {}
    for n, nm in (("q", "wq"), ("k", "wk"), ("kp", "wkp"), ("v", "wv"),
                  ("o", "wo")):
        w_src[n] = nc.dram_tensor(nm, [C, C], F32, kind="ExternalInput").ap()
    alpha_d = nc.dram_tensor("alpha", [1, 1], F32, kind="ExternalInput").ap()
    wgr_d = nc.dram_tensor("wg_red", [GD, C], F32, kind="ExternalInput").ap()
    wg_d = nc.dram_tensor("wg", [NE, GD], F32, kind="ExternalInput").ap()
    w1_d = nc.dram_tensor("w1", [NE, C, HID], F32, kind="ExternalInput").ap()
    b1_d = nc.dram_tensor("b1", [NE, HID], F32, kind="ExternalInput").ap()
    w2_d = nc.dram_tensor("w2", [NE, HID, C], F32, kind="ExternalInput").ap()
    b2_d = nc.dram_tensor("b2", [NE, C], F32, kind="ExternalInput").ap()
    out_d = nc.dram_tensor("out", [TOK, C], F32, kind="ExternalOutput").ap()

    xn_feat_d = nc.dram_tensor("xn_feat_d", [CCH, 128, TOK], BF16,
                               kind="Internal").ap()
    xn_tok_d = nc.dram_tensor("xn_tok_d", [TOK, C], BF16, kind="Internal").ap()
    s_tok_d = nc.dram_tensor("s_tok_d", [TOK, C], F32, kind="Internal").ap()
    o_d = nc.dram_tensor("o_d", [128, CCH, T], BF16, kind="Internal").ap()
    g_slot_d = nc.dram_tensor("g_slot_d", [NSLOT, C], F32,
                              kind="Internal").ap()
    inv_d = nc.dram_tensor("inv_d", [NSLOT, 1], U32, kind="Internal").ap()
    score_tok_d = nc.dram_tensor("score_tok_d", [TOK, 1], F32,
                                 kind="Internal").ap()
    wt_d = {n: nc.dram_tensor(f"wt_{n}_d", [128, CCH, C], BF16,
                              kind="Internal").ap()
            for n in ("q", "k", "kp", "v", "o")}
    w1pre_d = nc.dram_tensor("w1pre_d", [NE * 128, CCH * HID], BF16,
                             kind="Internal").ap()
    w2pre_d = nc.dram_tensor("w2pre_d", [NE * 128, HJ * C], BF16,
                             kind="Internal").ap()
    b1pre_d = nc.dram_tensor("b1pre_d", [NE * 128, HJ], F32,
                             kind="Internal").ap()

    ident_f32 = nc.inline_tensor(np.eye(128, dtype=np.float32), name="idf").ap()
    ident_bf16 = nc.inline_tensor(np.eye(128, dtype=ml_dtypes.bfloat16),
                                  name="idb").ap()
    ones_row = nc.inline_tensor(np.ones((1, 128), dtype=ml_dtypes.bfloat16),
                                name="onesr").ap()
    ones_row_f = nc.inline_tensor(np.ones((1, 128), dtype=np.float32),
                                  name="onesrf").ap()
    u_np = np.triu(np.ones((128, 128), dtype=ml_dtypes.bfloat16), 1)
    u_tri = nc.inline_tensor(u_np, name="utri").ap()
    eiota_row = nc.inline_tensor(
        np.arange(NE, dtype=np.float32).reshape(1, NE), name="eiota").ap()
    gstart_row = nc.inline_tensor(
        (np.arange(NG, dtype=np.float32) * G).reshape(1, NG),
        name="gstart").ap()

    sch = Sched()
    es0 = contextlib.ExitStack()

    def SB(stack, name, shape, dtype):
        return stack.enter_context(nc.sbuf_tensor(name, shape, dtype))

    def PSUM(stack, name, shape, dtype=F32):
        return stack.enter_context(nc.psum_tensor(name, shape, dtype))

    wt_names = ["q", "k", "kp", "v", "o"]
    score_sb = SB(es0, "score", [128, NTT], F32)
    idxf_sb = SB(es0, "idxf", [128, NTT], F32)
    score_buf, idxf_buf = Buf(), Buf()
    rank_u = SB(es0, "rank_u", [128, NTT], U32)
    rank_u_buf = Buf()
    idxw_sb = SB(es0, "idxw_sb", [128, NG], U32)
    idxb2_sb = SB(es0, "idxb2_sb", [128, NG], U32)
    idxw_buf, idxb2_buf = Buf(), Buf()
    alpha_sb = SB(es0, "alpha_sb", [1, 2], F32)
    alpha_buf = Buf()
    id_f_sb = SB(es0, "id_f_sb", [128, 128], F32)
    ones_f_sb = SB(es0, "ones_f_sb", [1, 128], F32)
    eps_sb = SB(es0, "eps_sb", [128, 1], F32)
    id_b_sb = SB(es0, "id_b_sb", [128, 128], BF16)
    ones_sb = SB(es0, "ones_sb", [1, 128], BF16)
    wgr_t = SB(es0, "wgr_t", [128, CCH, GD], F32)
    wgu_t = SB(es0, "wgu_t", [GD, NE], F32)
    wgr_t_buf, wgu_t_buf = Buf(), Buf()

    esA = contextlib.ExitStack()
    WTs = SB(esA, "wt_stage", [128, CCH, C], BF16)
    WTs_buf = Buf()
    a_tmp = SB(esA, "a_tmp", [1, 1], F32)
    wg_sb = SB(esA, "wg_sb", [NE, GD], F32)
    wgr_sb = SB(esA, "wgr_sb", [GD, C], F32)
    sch.dma("sync", "dc", id_f_sb[:], ident_f32[:, :])
    sch.dma("sync", "dc", id_b_sb[:], ident_bf16[:, :])
    sch.dma("sync", "dc", ones_sb[:], ones_row[:, :])
    sch.dma("sync", "dc", ones_f_sb[:], ones_row_f[:, :])
    sch.dma("sync", "dc", a_tmp[:], alpha_d[:, :])
    sch.dma("sync", "dc", wg_sb[:], wg_d[:, :])
    sch.dma("sync", "dc", wgr_sb[:], wgr_d[:, :])

    tok_i32 = SB(esA, "tok_i32", [128, NTT], I32)
    toku_sb = SB(esA, "toku_sb", [128, NTT], U32)
    piota_f = SB(esA, "piota_f", [128, NG], F32)
    u_sb = SB(esA, "u_sb", [128, 128], BF16)
    eiota_sb = SB(esA, "eiota_sb", [1, NE], F32)
    gstart_sb = SB(esA, "gstart_sb", [1, NG], F32)
    sch.dma("sync", "dc", u_sb[:], u_tri[:, :])
    sch.dma("sync", "dc", eiota_sb[:], eiota_row[:, :])
    sch.dma("sync", "dc", gstart_sb[:], gstart_row[:, :])
    eps_tok = sch.add("vector", lambda eng: eng.memset(eps_sb[:], 1e-5),
                      inc="dve")
    sch.add("vector", lambda eng: eng.tensor_copy(alpha_sb[:, 0:1], a_tmp[:]),
            waits=[("dc", 0)], inc="dve")
    t = sch.add("vector",
                lambda eng: eng.tensor_scalar(alpha_sb[:, 1:2], a_tmp[:],
                                              -1.0, 1.0, ALU.mult, ALU.add),
                waits=[("dc", 0)], inc="dve")
    alpha_buf.wrote(t)

    t = sch.add("gpsimd",
                lambda eng: eng.iota(tok_i32[:], pattern=[[128, NTT]], base=0,
                                     channel_multiplier=1),
                inc="pool")
    toku_tok = sch.add("vector",
                       lambda eng: eng.tensor_copy(toku_sb[:], tok_i32[:]),
                       waits=[t], inc="dve")
    pio_i = SB(esA, "pio_i", [128, NG], I32)
    t = sch.add("gpsimd",
                lambda eng: eng.iota(pio_i[:], pattern=[[0, NG]], base=0,
                                     channel_multiplier=1),
                inc="pool")
    pio_tok = sch.add("vector",
                      lambda eng: eng.tensor_copy(piota_f[:], pio_i[:]),
                      waits=[t], inc="dve")

    wg_sq = SB(esA, "wg_sq", [NE, GD], F32)
    wg_n = SB(esA, "wg_n", [NE, 4], F32)
    wgu_sb = SB(esA, "wgu_sb", [NE, GD], F32)
    ps_small = PSUM(esA, "ps_small", [128, 512], F32)
    ps_small_buf = Buf()
    t = sch.add("scalar", lambda eng: eng.activation(
        wg_sq[:], wg_sb[:], AF.Square, accum_out=wg_n[:, 0:1]),
        waits=[("dc", 0)], inc="act")
    t = sch.add("scalar", lambda eng: eng.activation(
        wg_n[:, 2:3], wg_n[:, 0:1], AF.Sqrt), waits=[t], inc="act")
    t = sch.add("vector", lambda eng: eng.tensor_scalar(
        wg_n[:, 2:3], wg_n[:, 2:3], 1e-4, None, ALU.max),
        waits=[(t[0], t[1], True)], inc="dve")
    t = sch.add("vector", lambda eng: eng.reciprocal(wg_n[:, 3:4],
                                                     wg_n[:, 2:3]),
                waits=[t], inc="dve")
    t = sch.add("vector", lambda eng: eng.tensor_scalar(
        wgu_sb[:], wg_sb[:], wg_n[:, 3:4], None, ALU.mult),
        waits=[t], inc="dve")
    tmm = sch.add("tensor", lambda eng: eng.transpose(
        ps_small[0:GD, 0:NE], wgu_sb[:], id_f_sb[0:NE, 0:NE]),
        waits=[t, ("dc", 0)], inc="pe")
    ps_small_buf.wrote(tmm)
    t = sch.add("vector", lambda eng: eng.tensor_copy(
        wgu_t[:], ps_small[0:GD, 0:NE]), waits=[tmm], inc="dve")
    wgu_t_buf.wrote(t)
    ps_small_buf.read(t)
    for c in range(CCH):
        tmm = sch.add("tensor", lambda eng, c=c: eng.transpose(
            ps_small[0:128, 0:GD], wgr_sb[:, c * 128:(c + 1) * 128],
            id_f_sb[0:GD, 0:GD]),
            waits=ps_small_buf.write_waits() + [("dc", 0)], inc="pe")
        ps_small_buf.wrote(tmm)
        t = sch.add("vector", lambda eng, c=c: eng.tensor_copy(
            wgr_t[:, c, :], ps_small[0:128, 0:GD]), waits=[tmm], inc="dve")
        ps_small_buf.read(t)
        wgr_t_buf.w = t

    wtmp = [SB(esA, f"wtmp{s}", [128, C], F32) for s in range(2)]
    wtmp_buf = [Buf(), Buf()]
    ps_trf = [PSUM(esA, f"ps_trf{s}", [128, 128], F32) for s in range(4)]
    ps_trf_buf = [Buf() for _ in range(4)]
    for wi, n in enumerate(wt_names):
        for i in range(CCH):
            sl = (wi * CCH + i) % 2
            dtok = sch.dma("sync", f"dwt{sl}", wtmp[sl][:],
                           w_src[n][i * 128:(i + 1) * 128, :],
                           waits=wtmp_buf[sl].write_waits())
            wtmp_buf[sl].wrote(dtok)
            for c in range(CCH):
                psl = c % 4
                tmm = sch.add("tensor", lambda eng, sl=sl, c=c, psl=psl:
                              eng.transpose(ps_trf[psl][:],
                                            wtmp[sl][:, c * 128:(c + 1) * 128],
                                            id_f_sb[:]),
                              waits=ps_trf_buf[psl].write_waits() +
                              [(f"dwt{sl}", 0), ("dc", 0)], inc="pe")
                ps_trf_buf[psl].wrote(tmm)
                wtmp_buf[sl].read(tmm)
                w_ws = [tmm]
                if i == 0 and c == 0:
                    w_ws += WTs_buf.write_waits()
                tcp = sch.add("vector", lambda eng, n=n, c=c, i=i, psl=psl:
                              eng.tensor_copy(
                                  WTs[:, c, i * 128:(i + 1) * 128],
                                  ps_trf[psl][:]), waits=w_ws, inc="dve")
                ps_trf_buf[psl].read(tcp)
                WTs_buf.w = tcp
        stw = sch.dma("sync", "dst", wt_d[n].rearrange("p c i -> p (c i)"),
                      WTs[:].rearrange("p c i -> p (c i)"),
                      waits=[WTs_buf.w])
        WTs_buf.read(stw)

    NIQpre = C // 256
    pcnt = 0
    for e in range(NE):
        sl = pcnt % 2
        pcnt += 1
        sch.dma("gpsimd", f"dp{sl}",
                w1pre_d[e * 128:(e + 1) * 128, :].rearrange(
                    "p (c j) -> p c j", c=CCH),
                w1_d[e].rearrange("(c p) j -> p c j", p=128))
    for e in range(NE):
        for iQ in range(NIQpre):
            sl = pcnt % 2
            pcnt += 1
            sch.dma("gpsimd", f"dp{sl}",
                    w2pre_d[e * 128:(e + 1) * 128,
                            iQ * HJ * 256:(iQ + 1) * HJ * 256].rearrange(
                        "p (j i) -> p j i", j=HJ),
                    w2_d[e].rearrange("(j p) i -> p j i", p=128)
                    [:, :, iQ * 256:(iQ + 1) * 256])
    b1_raw = SB(esA, "b1_raw", [HJ, 128], F32)
    b1_raw_buf = Buf()
    b1_t = SB(esA, "b1_t", [128, HJ], F32)
    b1_t_buf = Buf()
    ps_wtp = PSUM(esA, "ps_wtp", [128, 32], F32)
    ps_wtp_buf = Buf()
    for e in range(NE):
        ld = sch.dma("sync", "dx0", b1_raw[:],
                     b1_d[e].rearrange("(s p) -> s p", p=128),
                     waits=b1_raw_buf.write_waits())
        b1_raw_buf.wrote(ld)
        tm_ = sch.add("tensor", lambda eng: eng.transpose(
            ps_wtp[:, 0:HJ], b1_raw[:], id_f_sb[0:HJ, 0:HJ]),
            waits=[("dx0", 0), ("dc", 0)] + ps_wtp_buf.write_waits(),
            inc="pe")
        ps_wtp_buf.wrote(tm_)
        b1_raw_buf.read(tm_)
        tc_ = sch.add("vector", lambda eng: eng.tensor_copy(
            b1_t[:], ps_wtp[:, 0:HJ]),
            waits=[tm_] + b1_t_buf.write_waits(), inc="dve")
        ps_wtp_buf.read(tc_)
        b1_t_buf.wrote(tc_)
        st = sch.dma("sync", "dst", b1pre_d[e * 128:(e + 1) * 128, :],
                     b1_t[:], waits=[tc_])
        b1_t_buf.read(st)
    eiota_bc = SB(esA, "eiota_bc", [128, NE], F32)
    onescol = SB(esA, "onescol", [128, 1], BF16)
    oc_tok = sch.add("vector", lambda eng: eng.memset(onescol[:], 1.0),
                     inc="dve")
    tm_ = sch.add("tensor", lambda eng: eng.matmul(
        ps_wtp[:, 0:NE], ones_f_sb[0:1, :], eiota_sb[:],
        start=True, stop=True),
        waits=[("dc", 0)] + ps_wtp_buf.write_waits(), inc="pe")
    ps_wtp_buf.wrote(tm_)
    ebc_tok = sch.add("vector", lambda eng: eng.tensor_copy(
        eiota_bc[:], ps_wtp[:, 0:NE]), waits=[tm_], inc="dve")
    ps_wtp_buf.read(ebc_tok)
    oh_sb = SB(esA, "oh_sb", [128, NTT, NE], BF16)
    oh_buf = Buf()
    wtp_sb = SB(esA, "wtp_sb", [128, NTT, NE], F32)
    wtp_buf = Buf()
    hist_sb = SB(esA, "hist_sb", [1, NTT, NE], F32)
    hist_buf = Buf()

    xn_feat_sb = SB(esA, "xn_feat", [128, CCH, TOK], BF16)
    xn_feat_buf = Buf()
    NSL = 4
    x_t = [SB(esA, f"x_t{s}", [128, C], F32) for s in range(NSL)]
    x_t_buf = [Buf() for _ in range(NSL)]
    ln_sc = [SB(esA, f"ln_sc{s}", [128, 16], F32) for s in range(NSL)]
    sq_t1 = SB(esA, "sq_t", [128, C], F32)
    sq_t = [sq_t1] * NSL
    sq_common = Buf()
    sq_t_buf = [sq_common] * NSL
    xn_f = [SB(esA, f"xn_f{s}", [128, C], F32) for s in range(NSL)]
    xn_f_buf = [Buf() for _ in range(NSL)]
    xn_b = [SB(esA, f"xn_b{s}", [128, C], BF16) for s in range(NSL)]
    xn_b_buf = [Buf() for _ in range(NSL)]
    xnT_f = [SB(esA, f"xnT_f{s}", [128, CCH, 128], F32) for s in range(NSL)]
    xnT_f_buf = [Buf() for _ in range(NSL)]
    xr_sb = [SB(esA, f"xr_sb{s}", [GD, 128], F32) for s in range(NSL)]
    xr_buf = [Buf() for _ in range(NSL)]
    l_sb = [SB(esA, f"l_sb{s}", [128, NE], F32) for s in range(NSL)]
    l_buf = [Buf() for _ in range(NSL)]
    g_sc = [SB(esA, f"g_sc{s}", [128, 16], F32) for s in range(NSL)]
    gmax_buf = [Buf() for _ in range(NSL)]
    gtop_buf = [Buf() for _ in range(NSL)]
    gsum_buf = [Buf() for _ in range(NSL)]
    g_e8 = [SB(esA, f"g_e8{s}", [128, NE], F32) for s in range(NSL)]
    g_u8 = [SB(esA, f"g_u8{s}", [128, 8], U32) for s in range(NSL)]
    ps_xr = PSUM(esA, "ps_xr", [GD, 128], F32)
    ps_xr_buf = Buf()
    ps_lg = PSUM(esA, "ps_lg", [128, NE], F32)
    ps_lg_buf = Buf()

    for tt in range(NTT):
        sl = tt % NSL
        dtok = sch.dma("sync", f"dx{sl}", x_t[sl][:],
                       x_d[tt * 128:(tt + 1) * 128, :],
                       waits=x_t_buf[sl].write_waits())
        x_t_buf[sl].wrote(dtok)
        NBS = C // 512
        tbs = []
        for i in range(NBS):
            tbi = sch.add("vector", lambda eng, sl=sl, i=i: eng.bn_stats(
                ln_sc[sl][:, i * 6:(i + 1) * 6],
                x_t[sl][:, i * 512:(i + 1) * 512]),
                waits=[(f"dx{sl}", 0)], inc="dve")
            x_t_buf[sl].read(tbi)
            tbs.append(tbi)
        tb3 = sch.add("vector", lambda eng, sl=sl: eng.bn_aggr(
            ln_sc[sl][:, 12:14], ln_sc[sl][:, 0:6 * NBS]),
            waits=tbs, inc="dve")
        t7 = sch.add("scalar", lambda eng, sl=sl: eng.activation(
            ln_sc[sl][:, 6:7], ln_sc[sl][:, 13:14], AF.Ln,
            bias=eps_sb[:, 0:1]),
            waits=[(tb3[0], tb3[1], False), eps_tok], inc="act")
        t8 = sch.add("scalar", lambda eng, sl=sl: eng.activation(
            ln_sc[sl][:, 7:8], ln_sc[sl][:, 6:7], AF.Exp, scale=-0.5),
            waits=[(t7[0], t7[1], True)], inc="act")
        t9 = sch.add("vector", lambda eng, sl=sl: eng.scalar_tensor_tensor(
            ln_sc[sl][:, 0:1], ln_sc[sl][:, 12:13], -1.0, ln_sc[sl][:, 7:8],
            ALU.mult, ALU.mult), waits=[(t8[0], t8[1], False), tb3],
            inc="dve")
        t10 = sch.add("scalar", lambda eng, sl=sl: eng.activation(
            xn_f[sl][:], x_t[sl][:], AF.Identity, bias=ln_sc[sl][:, 0:1],
            scale=ln_sc[sl][:, 7:8]),
            waits=[(t9[0], t9[1], True), (f"dx{sl}", 0)] +
            xn_f_buf[sl].write_waits(), inc="act")
        xn_f_buf[sl].wrote(t10)
        x_t_buf[sl].read(t10)
        t11 = sch.add("vector", lambda eng, sl=sl: eng.tensor_copy(
            xn_b[sl][:], xn_f[sl][:]),
            waits=[(t10[0], t10[1], False)] + xn_b_buf[sl].write_waits(),
            inc="dve")
        xn_b_buf[sl].wrote(t11)
        xn_f_buf[sl].read(t11)

        for c in range(CCH):
            psl = c % 4
            w_ws = [t10] + ps_trf_buf[psl].write_waits()
            tmm = sch.add("tensor", lambda eng, sl=sl, c=c, psl=psl:
                          eng.transpose(ps_trf[psl][:],
                                        xn_f[sl][:, c * 128:(c + 1) * 128],
                                        id_f_sb[:]),
                          waits=w_ws, inc="pe")
            ps_trf_buf[psl].wrote(tmm)
            xn_f_buf[sl].read(tmm)
            w_ws = [tmm] + (xnT_f_buf[sl].write_waits() if c == 0 else [])
            tcv = sch.add("vector", lambda eng, sl=sl, c=c, psl=psl:
                          eng.tensor_copy(xnT_f[sl][:, c, :], ps_trf[psl][:]),
                          waits=w_ws, inc="dve")
            ps_trf_buf[psl].read(tcv)
            if c == 0:
                xnT_f_buf[sl].wrote(tcv)
            else:
                xnT_f_buf[sl].w = tcv
            tcb = sch.add("vector", lambda eng, tt=tt, c=c, psl=psl:
                          eng.tensor_copy(
                              xn_feat_sb[:, c, tt * 128:(tt + 1) * 128],
                              ps_trf[psl][:]), waits=[tmm], inc="dve")
            ps_trf_buf[psl].read(tcb)
            xn_feat_buf.w = tcb

        last = None
        for c in range(CCH):
            w_ws = [xnT_f_buf[sl].w, wgr_t_buf.w]
            if c == 0:
                w_ws += ps_xr_buf.write_waits()
            last = sch.add("tensor", lambda eng, sl=sl, c=c: eng.matmul(
                ps_xr[:], wgr_t[:, c, :], xnT_f[sl][:, c, :],
                start=(c == 0), stop=(c == CCH - 1)),
                waits=w_ws, inc="pe" if c == CCH - 1 else None)
        ps_xr_buf.wrote(last)
        xnT_f_buf[sl].read(last)
        t = sch.add("vector", lambda eng, sl=sl: eng.tensor_copy(
            xr_sb[sl][:], ps_xr[:]),
            waits=[last] + xr_buf[sl].write_waits(), inc="dve")
        xr_buf[sl].wrote(t)
        ps_xr_buf.read(t)
        tl = sch.add("tensor", lambda eng, sl=sl: eng.matmul(
            ps_lg[:], xr_sb[sl][:], wgu_t[:], start=True, stop=True),
            waits=[t, wgu_t_buf.w] + ps_lg_buf.write_waits(), inc="pe")
        ps_lg_buf.wrote(tl)
        xr_buf[sl].read(tl)
        tle = sch.add("vector", lambda eng, sl=sl: eng.tensor_copy(
            l_sb[sl][:], ps_lg[:]),
            waits=[tl] + l_buf[sl].write_waits(), inc="dve")
        l_buf[sl].wrote(tle)
        ps_lg_buf.read(tle)
        tm = sch.add("vector", lambda eng, sl=sl: eng.tensor_reduce(
            g_sc[sl][:, 0:1], l_sb[sl][:], AX.X, ALU.max, negate=True),
            waits=[tle] + gmax_buf[sl].write_waits(), inc="dve")
        gmax_buf[sl].wrote(tm)
        l_buf[sl].read(tm)
        t8m = sch.add("vector", lambda eng, sl=sl: eng.max(
            g_sc[sl][:, 8:16], l_sb[sl][:]),
            waits=[tle] + gtop_buf[sl].write_waits(), inc="dve")
        gtop_buf[sl].wrote(t8m)
        l_buf[sl].read(t8m)
        tix = sch.add("vector", lambda eng, sl=sl: eng.max_index(
            g_u8[sl][:], g_sc[sl][:, 8:16], l_sb[sl][:]),
            waits=[(t8m[0], t8m[1], True)], inc="dve")
        gtop_buf[sl].read(tix)
        l_buf[sl].read(tix)
        te = sch.add("scalar", lambda eng, sl=sl: eng.activation(
            g_e8[sl][:], l_sb[sl][:], AF.Exp, bias=g_sc[sl][:, 0:1],
            accum_out=g_sc[sl][:, 1:2]),
            waits=[tm, tle] + gsum_buf[sl].write_waits(), inc="act")
        gsum_buf[sl].wrote(te)
        gmax_buf[sl].read(te)
        l_buf[sl].read(te)
        ts = sch.add("vector", lambda eng, sl=sl, tt=tt: eng.reciprocal(
            score_sb[:, tt:tt + 1], g_sc[sl][:, 1:2]),
            waits=[(te[0], te[1], True)], inc="dve")
        gsum_buf[sl].read(ts)
        score_buf.w = ts
        tif = sch.add("vector", lambda eng, sl=sl, tt=tt: eng.tensor_copy(
            idxf_sb[:, tt:tt + 1], g_u8[sl][:, 0:1]),
            waits=[(tix[0], tix[1], True)], inc="dve")
        idxf_buf.w = tif

        toh = sch.add("vector", lambda eng, tt=tt: eng.tensor_scalar(
            oh_sb[:, tt, :], eiota_bc[:], idxf_sb[:, tt:tt + 1], None,
            ALU.is_equal), waits=[tif, ebc_tok] + oh_buf.readers,
            inc="dve")
        oh_buf.w = toh
        thm = sch.add("tensor", lambda eng, tt=tt: eng.matmul(
            ps_small[0:1, 0:NE], onescol[:], oh_sb[:, tt, :],
            start=True, stop=True),
            waits=[toh, oc_tok] + ps_small_buf.write_waits(), inc="pe")
        ps_small_buf.wrote(thm)
        oh_buf.readers.append(thm)
        the = sch.add("vector", lambda eng, tt=tt: eng.tensor_copy(
            hist_sb[:, tt, :], ps_small[0:1, 0:NE]),
            waits=[thm], inc="dve")
        ps_small_buf.read(the)
        hist_buf.w = the
        twm = sch.add("tensor", lambda eng, tt=tt: eng.matmul(
            ps_wtp[:, 0:NE], u_sb[:], oh_sb[:, tt, :],
            start=True, stop=True),
            waits=[toh, ("dc", 0)] + ps_wtp_buf.write_waits(), inc="pe")
        ps_wtp_buf.wrote(twm)
        oh_buf.readers.append(twm)
        twe = sch.add("vector", lambda eng, tt=tt: eng.tensor_copy(
            wtp_sb[:, tt, :], ps_wtp[:, 0:NE]),
            waits=[twm], inc="dve")
        ps_wtp_buf.read(twe)
        wtp_buf.w = twe
        stx = sch.dma("sync", "dst", xn_tok_d[tt * 128:(tt + 1) * 128, :],
                      xn_b[sl][:], waits=[t11])
        xn_b_buf[sl].read(stx)

    xn_store = sch.dma("sync", "dxn", xn_feat_d.rearrange("c p t -> p c t"),
                       xn_feat_sb[:], waits=[xn_feat_buf.w])
    xn_feat_buf.read(xn_store)

    cum_sb = SB(esA, "cum_sb", [1, NTT + 1, NE], F32)
    scr8 = SB(esA, "scr8", [128, NE], F32)
    scr8b = SB(esA, "scr8b", [128, NE], F32)
    rank_f = SB(esA, "rank_f", [128, 1], F32)
    cnt_u = SB(esA, "cnt_u", [1, NE], U32)
    po_u = SB(esA, "po_u", [1, NE + 1], U32)
    po_f = SB(esA, "po_f", [1, NE + 1], F32)
    grpe_f = SB(esA, "grpe_f", [1, NG], F32)
    sct_sb = SB(esA, "sct_sb", [NTT, 128], F32)
    tcum = sch.add("vector", lambda eng: eng.memset(cum_sb[:, 0, :], 0.0),
                   inc="dve")
    for tt in range(NTT):
        tcum = sch.add("vector", lambda eng, tt=tt: eng.tensor_add(
            cum_sb[:, tt + 1, :], cum_sb[:, tt, :], hist_sb[:, tt, :]),
            waits=[tcum, hist_buf.w], inc="dve")
    t1 = sch.add("vector", lambda eng: eng.tensor_copy(
        cnt_u[:], cum_sb[:, NTT, :]), waits=[tcum], inc="dve")
    t1 = sch.add("vector", lambda eng: eng.tensor_scalar(
        cnt_u[:], cnt_u[:], G - 1, None, ALU.add), waits=[t1], inc="dve")
    t1 = sch.add("vector", lambda eng: eng.tensor_scalar(
        cnt_u[:], cnt_u[:], 8, None, ALU.logical_shift_right),
        waits=[t1], inc="dve")
    t1 = sch.add("vector", lambda eng: eng.tensor_scalar(
        cnt_u[:], cnt_u[:], 8, None, ALU.logical_shift_left),
        waits=[t1], inc="dve")
    tpo = sch.add("vector", lambda eng: eng.memset(po_u[:, 0:1], 0),
                  inc="dve")
    for e in range(NE):
        tpo = sch.add("vector", lambda eng, e=e: eng.tensor_add(
            po_u[:, e + 1:e + 2], po_u[:, e:e + 1], cnt_u[:, e:e + 1]),
            waits=[tpo, t1], inc="dve")
    tpf = sch.add("vector", lambda eng: eng.tensor_copy(po_f[:], po_u[:]),
                  waits=[tpo], inc="dve")
    tg = sch.add("vector", lambda eng: eng.memset(grpe_f[:], 0.0),
                 inc="dve")
    for e in range(NE):
        tg = sch.add("vector", lambda eng, e=e: eng.scalar_tensor_tensor(
            grpe_f[:], gstart_sb[:], po_f[0:1, e + 1:e + 2], grpe_f[:],
            ALU.is_ge, ALU.add), waits=[tg, tpf, ("dc", 0)], inc="dve")
    tbm = sch.add("tensor", lambda eng: eng.matmul(
        ps_wtp[:, 0:NG], ones_f_sb[0:1, :], grpe_f[:],
        start=True, stop=True),
        waits=[tg, ("dc", 0)] + ps_wtp_buf.write_waits(), inc="pe")
    ps_wtp_buf.wrote(tbm)
    tie = sch.add("vector", lambda eng: eng.tensor_copy(
        idxb2_sb[:], ps_wtp[:, 0:NG]), waits=[tbm], inc="dve")
    idxb2_buf.wrote(tie)
    ps_wtp_buf.read(tie)
    tiw = sch.add("vector", lambda eng: eng.scalar_tensor_tensor(
        piota_f[:], ps_wtp[:, 0:NG], 128.0, piota_f[:], ALU.mult, ALU.add),
        waits=[tbm, pio_tok], inc="dve")
    ps_wtp_buf.read(tiw)
    tiw = sch.add("vector", lambda eng: eng.tensor_copy(
        idxw_sb[:], piota_f[:]), waits=[tiw], inc="dve")
    idxw_buf.wrote(tiw)
    tcb0 = None
    for tt in range(NTT):
        tcb0 = sch.add("vector", lambda eng, tt=tt: eng.tensor_add(
            cum_sb[:, tt, :], cum_sb[:, tt, :], po_f[0:1, 0:NE]),
            waits=[tpf, tcum] + ([] if tcb0 is None else [tcb0]),
            inc="dve")
    for tt in range(NTT):
        tbm = sch.add("tensor", lambda eng, tt=tt: eng.matmul(
            ps_wtp[:, 0:NE], ones_f_sb[0:1, :], cum_sb[:, tt, :],
            start=True, stop=True),
            waits=[tcb0] + ps_wtp_buf.write_waits(), inc="pe")
        ps_wtp_buf.wrote(tbm)
        tr1 = sch.add("vector", lambda eng, tt=tt: eng.tensor_add(
            scr8[:], wtp_sb[:, tt, :], ps_wtp[:, 0:NE]),
            waits=[tbm, wtp_buf.w], inc="dve")
        ps_wtp_buf.read(tr1)
        tr2 = sch.add("vector", lambda eng, tt=tt: eng.tensor_tensor(
            scr8b[:], scr8[:], oh_sb[:, tt, :], ALU.mult),
            waits=[tr1, oh_buf.w], inc="dve")
        oh_buf.readers.append(tr2)
        tr3 = sch.add("vector", lambda eng: eng.tensor_reduce(
            rank_f[:], scr8b[:], AX.X, ALU.add), waits=[tr2], inc="dve")
        tr4 = sch.add("vector", lambda eng, tt=tt: eng.tensor_copy(
            rank_u[:, tt:tt + 1], rank_f[:]),
            waits=[(tr3[0], tr3[1], True)], inc="dve")
        rank_u_buf.w = tr4
        tsc = sch.add("gpsimd", lambda eng, tt=tt: eng.indirect_dma_start(
            out=inv_d[:, :],
            out_offset=bass.IndirectOffsetOnAxis(
                ap=rank_u[:, tt:tt + 1], axis=0),
            in_=toku_sb[:, tt:tt + 1], in_offset=None),
            waits=[(tr4[0], tr4[1], False), toku_tok, ("dsc", 0)],
            inc="dsc", amt=16)
        rank_u_buf.readers.append(tsc)
    tsm = sch.add("tensor", lambda eng: eng.transpose(
        ps_small[0:NTT, 0:128], score_sb[:], id_f_sb[:]),
        waits=[score_buf.w, ("dc", 0)] + ps_small_buf.write_waits(),
        inc="pe")
    ps_small_buf.wrote(tsm)
    score_buf.read(tsm)
    tse = sch.add("vector", lambda eng: eng.tensor_copy(
        sct_sb[:], ps_small[0:NTT, 0:128]), waits=[tsm], inc="dve")
    ps_small_buf.read(tse)
    sts = sch.dma("sync", "dst",
                  score_tok_d.rearrange("(t p) o -> t (p o)", p=128),
                  sct_sb[:], waits=[tse])

    yield "A", sch, es0, esA, None
    sch.barrier(exclude=("dp0", "dp1"))

    esB = contextlib.ExitStack()
    NIQ = C // 256
    PJW = 512 if QT >= 512 else QT
    wts = {n: [SB(esB, f"wts_{n}{i}", [128, CCH, 128], BF16)
               for i in range(2)] for n in ("q", "k", "kp", "v")}
    wts_buf = {n: [Buf(), Buf()] for n in ("q", "k", "kp", "v")}
    wto_h = SB(esB, "wto_h", [128, CCH, PJW], BF16)
    wto_buf = Buf()
    xnb_sb = SB(esB, "xnb_sb", [128, CCH, T], BF16)
    xnb_buf = Buf()
    q_sb = SB(esB, "q_sb", [128, T], BF16)
    k_sb = SB(esB, "k_sb", [128, T], BF16)
    kp_sb = SB(esB, "kp_sb", [128, T], BF16)
    v_sb = SB(esB, "v_sb", [128, KC, 130], BF16)
    qkv_buf = {n: Buf() for n in ("q", "k", "kp", "v")}
    aT = [SB(esB, f"aT{s}", [128, 2, QT], BF16) for s in range(2)]
    aT_buf = [Buf(), Buf()]
    nrow_bf = SB(esB, "nrow_bf", [1, QT], BF16)
    nrow_buf = Buf()
    alpharow = SB(esB, "alpharow", [1, 2, 64], BF16)
    alpharow_buf = Buf()
    rc_sb = SB(esB, "rc_sb", [1, QT], F32)
    rc_buf = Buf()
    rc2_sb = SB(esB, "rc2_sb", [1, QT], F32)
    rc2_buf = Buf()
    nbc_sb = SB(esB, "nbc_sb", [128, QT], BF16)
    nbc_buf = Buf()
    otmp_sb = SB(esB, "otmp_sb", [128, QT], BF16)
    otmp_buf = Buf()
    o_stage = [SB(esB, f"o_stage{s}", [128, QT], BF16) for s in range(2)]
    o_stage_buf = [Buf(), Buf()]
    osl_sb = [SB(esB, f"osl_sb{s}", [128, CCH, 128], BF16) for s in range(2)]
    osl_buf = [Buf(), Buf()]
    s_stage = SB(esB, "s_stage", [128, C], F32)
    s_stage_buf = Buf()
    w1g = SB(esB, "w1g", [128, CCH, HID], BF16)
    w1g_buf = Buf()
    w2g = SB(esB, "w2g", [128, HJ, 256], BF16)
    w2g_buf = Buf()
    b1g = SB(esB, "b1g", [128, HJ], F32)
    b1g_buf = Buf()
    b2g = SB(esB, "b2g", [128, C], BF16)
    b2g_buf = Buf()
    invg = [SB(esB, f"invg{i}", [128, 1], U32) for i in range(2)]
    invg_buf = [Buf(), Buf()]
    xg_f = [SB(esB, f"xg_f{i}", [128, C], F32) for i in range(2)]
    xg_buf = [Buf(), Buf()]
    scg = [SB(esB, f"scg{i}", [128, 1], F32) for i in range(2)]
    scg_buf = [Buf(), Buf()]
    xgT = SB(esB, "xgT", [128, CCH, G], BF16)
    xgT_buf = Buf()
    h_sb = SB(esB, "h_sb", [128, HJ, G], BF16)
    h_buf = Buf()
    gs_sb = SB(esB, "gs_sb", [128, 256], F32)
    gs_buf = Buf()
    sc2 = [PSUM(esB, f"sc2_{i}", [128, 2, QT], F32) for i in range(2)]
    sc2_buf = [Buf(), Buf()]
    ps_o = [PSUM(esB, f"ps_o{i}", [65, QT], F32) for i in range(2)]
    ps_o_buf = [Buf(), Buf()]
    ps_pj = [PSUM(esB, f"ps_pj{i}", [128, PJW], F32) for i in range(2)]
    ps_pj_buf = [Buf(), Buf()]
    ps_h = ps_pj[1]
    ps_h_buf = ps_pj_buf[1]

    bnd_w = nc.gpsimd.alloc_register("bnd_w")
    bnd_e = nc.gpsimd.alloc_register("bnd_e")
    bnd_t = nc.gpsimd.alloc_register("bnd_t")
    tbw = sch.add("gpsimd", lambda eng: eng.reg_mov(bnd_w, NE * 128 - 1),
                  inc="pool")
    tbe = sch.add("gpsimd", lambda eng: eng.reg_mov(bnd_e, NE - 1),
                  inc="pool")
    tbt = sch.add("gpsimd", lambda eng: eng.reg_mov(bnd_t, TOK - 1),
                  inc="pool")

    def pj(i):
        return ps_pj[i][:, :]

    for _m in range(2):
        t_ar = sch.add("vector", lambda eng, _m=_m: eng.tensor_scalar(
            alpharow[:, _m, :], ones_sb[0:1, 0:64],
            alpha_sb[0:1, _m:_m + 1], None, ALU.mult),
            waits=[alpha_buf.w, ("dc", 0)], inc="dve")
        alpharow_buf.w = t_ar

    def moe_stream():
        for g in range(NG):
            for tm in range(2):
                ld = sch.dma("sync", f"dn{tm}", invg[tm][:],
                             inv_d[g * G + tm * 128:
                                   g * G + (tm + 1) * 128, :],
                             waits=invg_buf[tm].write_waits())
                invg_buf[tm].wrote(ld)
                gx = sch.add("gpsimd", lambda eng, tm=tm:
                             eng.indirect_dma_start(
                                 out=xg_f[tm][:], out_offset=None,
                                 in_=xn_tok_d[:, :],
                                 in_offset=bass.IndirectOffsetOnAxis(
                                     ap=invg[tm][:, 0:1], axis=0),
                                 bounds_check=bnd_t, oob_is_err=False),
                             waits=[(f"dn{tm}", 0), tbt] +
                             xg_buf[tm].write_waits(),
                             inc=f"dp{tm}", amt=16)
                xg_buf[tm].wrote(gx)
                invg_buf[tm].read(gx)
                gs_ = sch.add("gpsimd", lambda eng, tm=tm:
                              eng.indirect_dma_start(
                                  out=scg[tm][:], out_offset=None,
                                  in_=score_tok_d[:, :],
                                  in_offset=bass.IndirectOffsetOnAxis(
                                      ap=invg[tm][:, 0:1], axis=0),
                                  bounds_check=bnd_t, oob_is_err=False),
                              waits=[(f"dn{tm}", 0), tbt] +
                              scg_buf[tm].write_waits(),
                              inc=f"dp{tm}", amt=16)
                scg_buf[tm].wrote(gs_)
                invg_buf[tm].read(gs_)
            gw1 = sch.add("gpsimd", lambda eng, g=g: eng.indirect_dma_start(
                out=w1g[:].rearrange("p c j -> p (c j)"), out_offset=None,
                in_=w1pre_d[:, :],
                in_offset=bass.IndirectOffsetOnAxis(
                    ap=idxw_sb[:, g:g + 1], axis=0),
                bounds_check=bnd_w, oob_is_err=False),
                waits=[idxw_buf.w, tbw, ("dp0", 0), ("dp1", 0)] +
                w1g_buf.write_waits(),
                inc="dw0", amt=16)
            w1g_buf.wrote(gw1)
            idxw_buf.read(gw1)
            gb1 = sch.add("gpsimd", lambda eng, g=g: eng.indirect_dma_start(
                out=b1g[:], out_offset=None,
                in_=b1pre_d[:, :],
                in_offset=bass.IndirectOffsetOnAxis(
                    ap=idxw_sb[:, g:g + 1], axis=0),
                bounds_check=bnd_w, oob_is_err=False),
                waits=[idxw_buf.w, tbw] + b1g_buf.write_waits(),
                inc="dw2", amt=16)
            b1g_buf.wrote(gb1)
            idxw_buf.read(gb1)
            gb2 = sch.add("gpsimd", lambda eng, g=g: eng.indirect_dma_start(
                out=b2g[:], out_offset=None,
                in_=b2_d[:, :],
                in_offset=bass.IndirectOffsetOnAxis(
                    ap=idxb2_sb[:, g:g + 1], axis=0),
                bounds_check=bnd_e, oob_is_err=False),
                waits=[idxb2_buf.w, tbe] + b2g_buf.write_waits(),
                inc="dw2", amt=16)
            b2g_buf.wrote(gb2)
            idxb2_buf.read(gb2)
            yield
            for tm in range(2):
                for c in range(CCH):
                    tb = (tm * CCH + c) % 2
                    tmm = sch.add("tensor", lambda eng, tm=tm, c=c, tb=tb:
                                  eng.transpose(
                                      ps_pj[tb][:, 0:128],
                                      xg_f[tm][:, c * 128:(c + 1) * 128],
                                      id_f_sb[:]),
                                  waits=[xg_buf[tm].w, ("dc", 0)] +
                                  ps_pj_buf[tb].write_waits(), inc="pe")
                    ps_pj_buf[tb].wrote(tmm)
                    xg_buf[tm].read(tmm)
                    w_ws = [tmm]
                    if tm == 0 and c == 0:
                        w_ws += xgT_buf.write_waits()
                    tcv = sch.add("vector", lambda eng, tm=tm, c=c, tb=tb:
                                  eng.tensor_copy(
                                      xgT[:, c, tm * 128:(tm + 1) * 128],
                                      ps_pj[tb][:, 0:128]),
                                  waits=w_ws, inc="dve")
                    ps_pj_buf[tb].read(tcv)
                    if tm == 0 and c == 0:
                        xgT_buf.wrote(tcv)
                    else:
                        xgT_buf.w = tcv
            yield
            for j in range(HJ):
                hb = j % 2
                last = None
                for c in range(CCH):
                    w_ws = [w1g_buf.w, xgT_buf.w]
                    if c == 0:
                        w_ws += ps_pj_buf[hb].write_waits()
                    last = sch.add("tensor", lambda eng, j=j, c=c, hb=hb:
                                   eng.matmul(
                                       ps_pj[hb][:, 0:G],
                                       w1g[:, c, j * 128:(j + 1) * 128],
                                       xgT[:, c, :],
                                       start=(c == 0), stop=(c == CCH - 1)),
                                   waits=w_ws,
                                   inc="pe" if c == CCH - 1 else None)
                ps_pj_buf[hb].wrote(last)
                w1g_buf.read(last)
                xgT_buf.read(last)
                w_ws = [last, b1g_buf.w] + (h_buf.write_waits() if j == 0
                                            else [])
                tg_ = sch.add("vector", lambda eng, j=j, hb=hb:
                              eng.tensor_scalar(h_sb[:, j, :],
                                                ps_pj[hb][:, 0:G],
                                                b1g[:, j:j + 1], None,
                                                ALU.add),
                              waits=w_ws, inc="dve")
                ps_pj_buf[hb].read(tg_)
                b1g_buf.read(tg_)
                if j == 0:
                    h_buf.wrote(tg_)
                else:
                    h_buf.w = tg_
                yield
            for jb in range((HJ + 7) // 8):
                hv = h_sb[:, jb * 8:min((jb + 1) * 8, HJ), :].rearrange(
                    "p a b -> p (a b)")
                tg_ = sch.add("scalar", lambda eng, hv=hv: eng.activation(
                    hv, hv, AF.Gelu),
                    waits=[h_buf.w], inc="act")
                h_buf.w = tg_
            yield
            for iQ in range(NIQ):
                gw2 = sch.add("gpsimd", lambda eng, g=g, iQ=iQ:
                              eng.indirect_dma_start(
                                  out=w2g[:].rearrange("p j i -> p (j i)"),
                                  out_offset=None,
                                  in_=w2pre_d[:, :],
                                  in_offset=bass.IndirectOffsetOnAxis(
                                      ap=idxw_sb[:, g:g + 1], axis=0),
                                  element_offset=iQ * HJ * 256,
                                  bounds_check=bnd_w, oob_is_err=False),
                              waits=[idxw_buf.w, tbw, ("dp0", 0),
                                     ("dp1", 0)] +
                              w2g_buf.write_waits(),
                              inc="dw1", amt=16)
                w2g_buf.wrote(gw2)
                idxw_buf.read(gw2)
                for tm in range(2):
                    last = None
                    for j in range(HJ):
                        w_ws = [h_buf.w, w2g_buf.w]
                        if j == 0:
                            w_ws += ps_pj_buf[0].write_waits()
                        last = sch.add("tensor", lambda eng, j=j, tm=tm:
                                       eng.matmul(
                                           ps_pj[0][:, 0:256],
                                           h_sb[:, j,
                                                tm * 128:(tm + 1) * 128],
                                           w2g[:, j, :],
                                           start=(j == 0), stop=False),
                                       waits=w_ws, inc=None)
                    last = sch.add("tensor", lambda eng, iQ=iQ:
                                   eng.matmul(
                                       ps_pj[0][:, 0:256], ones_sb[:],
                                       b2g[0:1, iQ * 256:(iQ + 1) * 256],
                                       start=False, stop=True),
                                   waits=[b2g_buf.w, ("dc", 0)], inc="pe")
                    ps_pj_buf[0].wrote(last)
                    h_buf.read(last)
                    w2g_buf.read(last)
                    b2g_buf.read(last)
                    tev = sch.add("scalar", lambda eng, tm=tm:
                                  eng.activation(
                                      gs_sb[:], ps_pj[0][:, 0:256], AF.Silu,
                                      scale=scg[tm][:, 0:1]),
                                  waits=[last, scg_buf[tm].w] +
                                  gs_buf.write_waits(), inc="act")
                    gs_buf.wrote(tev)
                    ps_pj_buf[0].read(tev)
                    scg_buf[tm].read(tev)
                    st = sch.dma("sync", "dst",
                                 g_slot_d[g * G + tm * 128:
                                          g * G + (tm + 1) * 128,
                                          iQ * 256:(iQ + 1) * 256],
                                 gs_sb[:], waits=[tev])
                    gs_buf.read(st)
                yield

    moe_gen = moe_stream()
    moe_done = [False]
    n_yields = NG * (6 + HJ + NIQ) + 2
    n_a_sites = B * HC * NQT * 3 + B * (C // PJW) * (T // 128)
    a_every = max(1, (n_a_sites * 4) // (3 * n_yields))
    n_b_sites = B * HC * NQT * 2 * KC
    b_every = max(1, (n_b_sites * 10) // (n_yields * 9))
    actr = [0]
    bctr = [0]

    def pump(n=1, force=False):
        if moe_done[0]:
            return
        if not force:
            bctr[0] += 1
            if bctr[0] % b_every != 0:
                return
        for _ in range(n):
            try:
                next(moe_gen)
            except StopIteration:
                moe_done[0] = True
                return

    def pump_a():
        if moe_done[0]:
            return
        actr[0] += 1
        if actr[0] % a_every != 0:
            return
        try:
            next(moe_gen)
        except StopIteration:
            moe_done[0] = True

    def proj_feat(dst, dbuf, wname, hc, sl):
        for ttt in range(T // PJW):
            pps = ttt % 2
            last = None
            for c in range(CCH):
                w_ws = [xnb_buf.w, wts_buf[wname][sl].w]
                if c == 0:
                    w_ws += ps_pj_buf[pps].write_waits()
                last = sch.add("tensor", lambda eng, c=c, hc=hc, ttt=ttt,
                               wname=wname, pps=pps, sl=sl: eng.matmul(
                                   pj(pps),
                                   wts[wname][sl][:, c, :],
                                   xnb_sb[:, c, ttt * PJW:(ttt + 1) * PJW],
                                   start=(c == 0), stop=(c == CCH - 1)),
                               waits=w_ws,
                               inc="pe" if c == CCH - 1 else None)
            ps_pj_buf[pps].wrote(last)
            xnb_buf.read(last)
            wts_buf[wname][sl].read(last)
            w_ws = [last] + (dbuf.write_waits() if ttt == 0 else [])
            tcp = sch.add("vector", lambda eng, ttt=ttt, pps=pps,
                          dst=dst: eng.tensor_copy(
                              dst[:, ttt * PJW:(ttt + 1) * PJW],
                              pj(pps)), waits=w_ws, inc="dve")
            ps_pj_buf[pps].read(tcp)
            if ttt == 0:
                dbuf.wrote(tcp)
            else:
                dbuf.w = tcp

    for b in range(B):
        dtok = sch.dma("sync", "dxn", xnb_sb[:],
                       xn_feat_d[:, :, b * T:(b + 1) * T].rearrange(
                           "c p t -> p c t"),
                       waits=xnb_buf.write_waits())
        xnb_buf.wrote(dtok)
        tmo = sch.add("vector", lambda eng: eng.memset(
            v_sb[:, :, :].rearrange("p t (a c) -> p t a c", a=2)
            [:, :, :, 64:65], 1.0),
            waits=qkv_buf["v"].write_waits(), inc="dve")
        qkv_buf["v"].wrote(tmo)

        def load_wts(hc):
            sl = hc % 2
            for n in ("q", "k", "kp", "v"):
                ldw = sch.dma("sync", f"dwt{sl}", wts[n][sl][:],
                              wt_d[n][:, :, hc * 128:(hc + 1) * 128],
                              waits=wts_buf[n][sl].write_waits() +
                              ([("dst", 0)] if b == 0 and hc < 2 else []))
                wts_buf[n][sl].wrote(ldw)

        if b == 0:
            load_wts(0)
        for hc in range(HC):
            sl = hc % 2
            if hc + 1 < HC:
                load_wts(hc + 1)
            elif b == 0:
                load_wts(0)
            proj_feat(q_sb, qkv_buf["q"], "q", hc, sl)
            proj_feat(k_sb, qkv_buf["k"], "k", hc, sl)
            proj_feat(kp_sb, qkv_buf["kp"], "kp", hc, sl)
            for tc_i in range(KC):
                pps = tc_i % 2
                last = None
                for c in range(CCH):
                    w_ws = [xnb_buf.w, wts_buf["v"][sl].w]
                    if c == 0:
                        w_ws += ps_pj_buf[pps].write_waits()
                    last = sch.add("tensor", lambda eng, c=c, hc=hc,
                                   tc_i=tc_i, pps=pps, sl=sl: eng.matmul(
                                       ps_pj[pps][:, 0:128],
                                       xnb_sb[:, c,
                                              tc_i * 128:(tc_i + 1) * 128],
                                       wts["v"][sl][:, c, :],
                                       start=(c == 0), stop=(c == CCH - 1)),
                                   waits=w_ws,
                                   inc="pe" if c == CCH - 1 else None)
                ps_pj_buf[pps].wrote(last)
                xnb_buf.read(last)
                wts_buf["v"][sl].read(last)
                w_ws = [last, qkv_buf["v"].w]
                if tc_i == 0:
                    w_ws += qkv_buf["v"].readers
                tv = sch.add("vector", lambda eng, tc_i=tc_i, pps=pps:
                             eng.tensor_copy(
                                 v_sb[:, tc_i, :].rearrange(
                                     "p (a c) -> p a c", a=2)[:, :, 0:64],
                                 ps_pj[pps][:, 0:128].rearrange(
                                     "p (a c) -> p a c", a=2)),
                             waits=w_ws, inc="dve")
                ps_pj_buf[pps].read(tv)
                if tc_i == 0:
                    qkv_buf["v"].readers = []
                qkv_buf["v"].w = tv

            for qt in range(NQT):
                osl = (hc * NQT + qt) % 2
                for mat in range(2):
                    ksrc, kn = (k_sb, "k") if mat == 0 else (kp_sb, "kp")

                    def emit_scores(kc, ksrc=None, kn=None, qt=qt):
                        par = kc % 2
                        smm = None
                        for h2 in range(2):
                            w_ws = [qkv_buf["q"].w, qkv_buf[kn].w]
                            if h2 == 0:
                                w_ws += sc2_buf[par].write_waits()
                            smm = sch.add("tensor", lambda eng, kc=kc,
                                          qt=qt, ksrc=ksrc, h2=h2, par=par:
                                          eng.matmul(
                                              sc2[par][:, h2, :],
                                              ksrc[h2 * 64:(h2 + 1) * 64,
                                                   kc * 128:(kc + 1) * 128],
                                              q_sb[h2 * 64:(h2 + 1) * 64,
                                                   qt * QT:(qt + 1) * QT],
                                              start=True, stop=True,
                                              tile_position=(h2 * 64, 0)),
                                          waits=w_ws, inc="pe")
                            qkv_buf["q"].read(smm)
                            qkv_buf[kn].read(smm)
                        sc2_buf[par].wrote(smm)
                        return smm

                    smm_next = emit_scores(0, ksrc=ksrc, kn=kn)
                    for kc in range(KC):
                        par = kc % 2
                        smm = smm_next
                        if kc + 1 < KC:
                            smm_next = emit_scores(kc + 1, ksrc=ksrc, kn=kn)
                        tex = sch.add("scalar", lambda eng, par=par:
                                      eng.activation(
                                          aT[par][:], sc2[par][:], AF.Exp,
                                          scale=0.125),
                                      waits=[smm] +
                                      aT_buf[par].write_waits(), inc="act")
                        aT_buf[par].wrote(tex)
                        sc2_buf[par].read(tex)
                        for h2 in range(2):
                            avw = [tex, qkv_buf["v"].w]
                            if kc == 0:
                                avw += ps_o_buf[h2].write_waits()
                            tav = sch.add("tensor", lambda eng, kc=kc,
                                          h2=h2, par=par: eng.matmul(
                                              ps_o[h2][:],
                                              v_sb[:, kc,
                                                   h2 * 65:(h2 + 1) * 65],
                                              aT[par][:, h2, :],
                                              start=(kc == 0),
                                              stop=(kc == KC - 1)),
                                          waits=avw,
                                          inc="pe" if kc == KC - 1
                                          else None)
                            if kc == KC - 1:
                                ps_o_buf[h2].wrote(tav)
                            if tav is not None:
                                aT_buf[par].read(tav)
                                qkv_buf["v"].read(tav)
                        if kc == KC - 1:
                            for h2 in range(2):
                                aT_buf[par].read(ps_o_buf[h2].w)
                                qkv_buf["v"].read(ps_o_buf[h2].w)
                        pump(1)
                    pump_a()
                    for h2 in range(2):
                        tln = sch.add("scalar", lambda eng, h2=h2:
                                      eng.activation(
                                          rc_sb[:], ps_o[h2][64:65, :],
                                          AF.Ln),
                                      waits=[ps_o_buf[h2].w] +
                                      rc_buf.write_waits(), inc="act")
                        rc_buf.wrote(tln)
                        ps_o_buf[h2].read(tln)
                        tnr = sch.add("scalar", lambda eng:
                                      eng.activation(
                                          nrow_bf[:], rc_sb[:], AF.Exp,
                                          scale=-1.0),
                                      waits=[(tln[0], tln[1], True)] +
                                      nrow_buf.write_waits(), inc="act")
                        nrow_buf.wrote(tnr)
                        rc_buf.read(tnr)
                        tbc = sch.add("tensor", lambda eng, mat=mat:
                                      eng.matmul(
                                          ps_pj[0][0:64, 0:QT],
                                          alpharow[0:1, mat, :],
                                          nrow_bf[:], start=True, stop=True),
                                      waits=[tnr, alpharow_buf.w] +
                                      ps_pj_buf[0].write_waits(),
                                      inc="pe")
                        ps_pj_buf[0].wrote(tbc)
                        nrow_buf.read(tbc)
                        tev = sch.add("vector", lambda eng, h2=h2:
                                      eng.tensor_copy(
                                          nbc_sb[h2 * 64:(h2 + 1) * 64, :],
                                          ps_pj[0][0:64, 0:QT]),
                                      waits=[tbc] + nbc_buf.write_waits(),
                                      inc="dve")
                        nbc_buf.wrote(tev)
                        ps_pj_buf[0].read(tev)
                        if mat == 0:
                            tcb = sch.add("vector", lambda eng, h2=h2,
                                          osl=osl: eng.tensor_tensor(
                                              o_stage[osl][
                                                  h2 * 64:(h2 + 1) * 64, :],
                                              ps_o[h2][0:64, :],
                                              nbc_sb[h2 * 64:
                                                     (h2 + 1) * 64, :],
                                              ALU.mult),
                                          waits=[tev, ps_o_buf[h2].w] +
                                          (o_stage_buf[osl].write_waits()
                                           if h2 == 0 else []), inc="dve")
                            if h2 == 0:
                                o_stage_buf[osl].wrote(tcb)
                            else:
                                o_stage_buf[osl].w = tcb
                            ps_o_buf[h2].read(tcb)
                            nbc_buf.read(tcb)
                        else:
                            tcb = sch.add("vector", lambda eng, h2=h2:
                                          eng.tensor_tensor(
                                              otmp_sb[h2 * 64:
                                                      (h2 + 1) * 64, :],
                                              ps_o[h2][0:64, :],
                                              nbc_sb[h2 * 64:
                                                     (h2 + 1) * 64, :],
                                              ALU.mult),
                                          waits=[tev, ps_o_buf[h2].w] +
                                          otmp_buf.write_waits(),
                                          inc="dve")
                            otmp_buf.wrote(tcb)
                            ps_o_buf[h2].read(tcb)
                            nbc_buf.read(tcb)
                            tad = sch.add("vector", lambda eng, h2=h2,
                                          osl=osl: eng.tensor_tensor(
                                              o_stage[osl][
                                                  h2 * 64:(h2 + 1) * 64, :],
                                              o_stage[osl][
                                                  h2 * 64:(h2 + 1) * 64, :],
                                              otmp_sb[h2 * 64:
                                                      (h2 + 1) * 64, :],
                                              ALU.add),
                                          waits=[tcb, o_stage_buf[osl].w],
                                          inc="dve")
                            o_stage_buf[osl].w = tad
                            otmp_buf.read(tad)
                sto = sch.dma("sync", "dst", o_d[:, hc, qt * QT:(qt + 1) * QT],
                              o_stage[osl][:], waits=[o_stage_buf[osl].w])
                o_stage_buf[osl].read(sto)
                pump_a()

        for iN in range(C // PJW):
            ldo = sch.dma("sync", "dwto", wto_h[:],
                          wt_d["o"][:, :, iN * PJW:(iN + 1) * PJW],
                          waits=wto_buf.write_waits())
            wto_buf.wrote(ldo)
            for ts_i in range(T // 128):
                ssl = ts_i % 2
                dso = sch.dma("sync", "dxn", osl_sb[ssl][:],
                              o_d[:, :, ts_i * 128:(ts_i + 1) * 128],
                              waits=osl_buf[ssl].write_waits() +
                              [("dst", 0)])
                osl_buf[ssl].wrote(dso)
                pps = 0
                last = None
                for c in range(HC):
                    w_ws = [osl_buf[ssl].w, wto_buf.w, ("dxn", 0)]
                    if c == 0:
                        w_ws += ps_pj_buf[pps].write_waits()
                    last = sch.add("tensor", lambda eng, ts_i=ts_i, iN=iN,
                                   c=c, pps=pps, ssl=ssl: eng.matmul(
                                       pj(pps),
                                       osl_sb[ssl][:, c, :],
                                       wto_h[:, c, :],
                                       start=(c == 0), stop=(c == HC - 1)),
                                   waits=w_ws,
                                   inc="pe" if c == HC - 1 else None)
                ps_pj_buf[pps].wrote(last)
                osl_buf[ssl].read(last)
                wto_buf.read(last)
                w_ws = [last] + s_stage_buf.write_waits()
                tse = sch.add("vector", lambda eng, iN=iN, pps=pps:
                              eng.tensor_copy(
                                  s_stage[:, iN * PJW:(iN + 1) * PJW],
                                  pj(pps)), waits=w_ws, inc="dve")
                ps_pj_buf[pps].read(tse)
                s_stage_buf.wrote(tse)
                dstore = sch.dma("sync", "dst",
                                 s_tok_d[b * T + ts_i * 128:
                                         b * T + (ts_i + 1) * 128,
                                         iN * PJW:(iN + 1) * PJW],
                                 s_stage[:, iN * PJW:(iN + 1) * PJW],
                                 waits=[s_stage_buf.w])
                s_stage_buf.read(dstore)
                pump_a()

    while not moe_done[0]:
        pump(1, force=True)

    yield "B", sch, es0, esB, None
    sch.barrier()

    esD = contextlib.ExitStack()
    NDS = 4
    f_x = [SB(esD, f"f_x{s}", [128, C], F32) for s in range(NDS)]
    f_g = [SB(esD, f"f_g{s}", [128, C], F32) for s in range(NDS)]
    f_s = [SB(esD, f"f_s{s}", [128, C], F32) for s in range(NDS)]
    f_o = [SB(esD, f"f_o{s}", [128, C], F32) for s in range(NDS)]
    f_bufs = {n: [Buf() for _ in range(NDS)] for n in "xgso"}
    for tt in range(NTT):
        sl = tt % NDS
        dx_ = sch.dma("sync", f"dx{sl}", f_x[sl][:],
                      x_d[tt * 128:(tt + 1) * 128, :],
                      waits=f_bufs["x"][sl].write_waits())
        f_bufs["x"][sl].wrote(dx_)
        dg_ = sch.add("gpsimd", lambda eng, sl=sl, tt=tt:
                      eng.indirect_dma_start(
                          out=f_g[sl][:], out_offset=None,
                          in_=g_slot_d[:, :],
                          in_offset=bass.IndirectOffsetOnAxis(
                              ap=rank_u[:, tt:tt + 1], axis=0)),
                      waits=[rank_u_buf.w] + f_bufs["g"][sl].write_waits(),
                      inc=f"dg{sl}", amt=16)
        f_bufs["g"][sl].wrote(dg_)
        ds_ = sch.dma("sync", f"dx{sl}", f_s[sl][:],
                      s_tok_d[tt * 128:(tt + 1) * 128, :],
                      waits=f_bufs["s"][sl].write_waits())
        f_bufs["s"][sl].wrote(ds_)
        tmu = sch.add("vector", lambda eng, sl=sl: eng.tensor_tensor(
            f_g[sl][:], f_g[sl][:], f_s[sl][:], ALU.mult),
            waits=[dg_, (f"dg{sl}", 0), (f"dx{sl}", 0)], inc="dve")
        f_bufs["g"][sl].w = tmu
        f_bufs["s"][sl].read(tmu)
        tad = sch.add("vector", lambda eng, sl=sl: eng.tensor_tensor(
            f_o[sl][:], f_x[sl][:], f_g[sl][:], ALU.add),
            waits=[tmu, (f"dx{sl}", 0)] + f_bufs["o"][sl].write_waits(),
            inc="dve")
        f_bufs["o"][sl].wrote(tad)
        f_bufs["x"][sl].read(tad)
        f_bufs["g"][sl].read(tad)
        do_ = sch.dma("sync", "dst", out_d[tt * 128:(tt + 1) * 128, :],
                      f_o[sl][:], waits=[tad])
        f_bufs["o"][sl].read(do_)

    sch.add("sync", lambda eng: eng.nop(),
            waits=[("dst", 0), ("dsc", 0), ("dx0", 0), ("dx1", 0),
                   ("dx2", 0), ("dx3", 0), ("dg0", 0), ("dg1", 0),
                   ("dg2", 0), ("dg3", 0), ("dxn", 0)])

    yield "D", sch, es0, esD, None


def build(cfg):
    nc = bass.Bass("TRN2", target_bir_lowering=False, debug=False)
    gen = build_program(cfg, nc)
    with contextlib.ExitStack() as outer:
        first = next(gen)
        sems = {s: outer.enter_context(nc.semaphore(s)) for s in ALL_SEMS}
        with nc.Block() as block:
            phase, sch, es0, es_phase, extra = first
            sch.emit(block, sems)
            es_phase.close()
            if extra is not None:
                extra.close()
            for phase, sch, es0, es_phase, extra in gen:
                sch.emit(block, sems)
                es_phase.close()
                if extra is not None:
                    extra.close()
            es0.close()
    return nc


def kernel(L0_attn_x, L0_attn_W_Q, L0_attn_W_K, L0_attn_W_V,
           L0_attn_W_K_prime, L0_attn_W_O, L0_attn_alpha,
           L0_ffn_wg_red, L0_ffn_wg, L0_ffn_w1, L0_ffn_b1, L0_ffn_w2,
           L0_ffn_b2):
    x = np.ascontiguousarray(np.asarray(L0_attn_x, dtype=np.float32))
    Bfull, T, C = x.shape
    per = Bfull // N_CORES
    shared = {
        "wq": np.ascontiguousarray(np.asarray(L0_attn_W_Q, dtype=np.float32)),
        "wk": np.ascontiguousarray(np.asarray(L0_attn_W_K, dtype=np.float32)),
        "wkp": np.ascontiguousarray(
            np.asarray(L0_attn_W_K_prime, dtype=np.float32)),
        "wv": np.ascontiguousarray(np.asarray(L0_attn_W_V, dtype=np.float32)),
        "wo": np.ascontiguousarray(np.asarray(L0_attn_W_O, dtype=np.float32)),
        "alpha": np.asarray(L0_attn_alpha, dtype=np.float32).reshape(1, 1),
        "wg_red": np.ascontiguousarray(
            np.asarray(L0_ffn_wg_red, dtype=np.float32)),
        "wg": np.ascontiguousarray(np.asarray(L0_ffn_wg, dtype=np.float32)),
        "w1": np.ascontiguousarray(np.asarray(L0_ffn_w1, dtype=np.float32)),
        "b1": np.ascontiguousarray(np.asarray(L0_ffn_b1, dtype=np.float32)),
        "w2": np.ascontiguousarray(np.asarray(L0_ffn_w2, dtype=np.float32)),
        "b2": np.ascontiguousarray(np.asarray(L0_ffn_b2, dtype=np.float32)),
    }
    in_maps = []
    for c in range(N_CORES):
        m = dict(shared)
        m["x"] = np.ascontiguousarray(
            x[c * per:(c + 1) * per].reshape(per * T, C))
        in_maps.append(m)
    nc = build(FULL_CFG)
    res = run_bass_kernel_spmd(nc, in_maps, core_ids=list(range(N_CORES)))
    outs = [np.asarray(r["out"]).reshape(per, T, C) for r in res.results]
    return np.concatenate(outs, axis=0)


# version 24
# speedup vs baseline: 1.1564x; 1.1564x over previous
import contextlib
import numpy as np
import ml_dtypes
import concourse.bass as bass
import concourse.mybir as mybir
from concourse.bass_utils import run_bass_kernel_spmd

F32 = mybir.dt.float32
BF16 = mybir.dt.bfloat16
U32 = mybir.dt.uint32
I32 = mybir.dt.int32
AF = mybir.ActivationFunctionType
ALU = mybir.AluOpType
AX = mybir.AxisListType

N_CORES = 8

COMPUTE_OWNER = {"pe": "tensor", "act": "scalar", "dve": "vector"}
DMA_SEMS = ("dc", "dx0", "dx1", "dx2", "dx3", "dw0", "dw1", "dw2", "dn0",
            "dn1", "dp0", "dp1", "dg0", "dg1", "dg2", "dg3", "dwt0",
            "dwt1", "dwto", "dst", "dsc", "dxn")
ALL_SEMS = ["pe", "act", "dve", "pool"] + list(DMA_SEMS)


class Sched:
    ENGINES = ("tensor", "vector", "scalar", "gpsimd", "sync")

    def __init__(self):
        self.ops = []
        self.counts = {s: 0 for s in ALL_SEMS}
        self.emitted_upto = 0
        self.pending_barrier = {}

    def add(self, engine, fn, waits=(), inc=None, amt=1):
        ws = []
        for w in waits:
            if w is None:
                continue
            if len(w) == 2:
                s, v = w
                force = False
            else:
                s, v, force = w
            if s in DMA_SEMS:
                v = self.counts[s]
            if v <= 0:
                continue
            ws.append((s, v, force))
        if self.pending_barrier.get(engine):
            ws.extend(self.pending_barrier.pop(engine))
        tok = None
        if inc is not None:
            self.counts[inc] += amt
            tok = (inc, self.counts[inc])
        self.ops.append((engine, fn, tuple(ws), inc, amt))
        return tok

    def dma(self, queue_engine, sem, out, in_, waits=()):
        def fn(eng, out=out, in_=in_):
            return eng.dma_start(out=out, in_=in_)
        return self.add(queue_engine, fn, waits=waits, inc=sem, amt=16)

    def barrier(self, exclude=()):
        snap = [(s, v, False) for s, v in self.counts.items()
                if v > 0 and s not in exclude]
        for e in self.ENGINES:
            self.pending_barrier[e] = list(snap)

    def emit(self, block, sems):
        ops = self.ops[self.emitted_upto:]
        self.emitted_upto = len(self.ops)
        for eng_name in self.ENGINES:
            eops = [(fn, ws, inc, amt) for (e, fn, ws, inc, amt) in ops
                    if e == eng_name]
            if not eops:
                continue

            def body(eng, eops=eops, eng_name=eng_name):
                hw = {}
                for (fn, ws, inc, amt) in eops:
                    for (s, v, force) in ws:
                        if hw.get(s, 0) >= v:
                            continue
                        eng.wait_ge(sems[s], v)
                        hw[s] = v
                    inst = fn(eng)
                    if inc is not None:
                        inst.then_inc(sems[inc], amt)
            getattr(block, eng_name)(body)


class Buf:
    def __init__(self):
        self.w = None
        self.readers = []

    def write_waits(self):
        ws = list(self.readers)
        if self.w is not None:
            ws.append(self.w)
        return ws

    def wrote(self, tok):
        self.w = tok
        self.readers = []

    def read(self, tok):
        if tok is not None:
            self.readers.append(tok)


FULL_CFG = dict(B=2, T=2048, CCH=8, NE=8, HJ=32, GD=16)


def build_program(cfg, nc):
    B = cfg["B"]
    T = cfg["T"]
    CCH = cfg["CCH"]
    C = CCH * 128
    HC = CCH
    NE = cfg["NE"]
    HJ = cfg["HJ"]
    HID = HJ * 128
    GD = cfg["GD"]
    TOK = B * T
    NTT = TOK // 128
    QT = min(512, T)
    NQT = T // QT
    KC = T // 128
    G = 256
    NG = (TOK + NE * (G - 1) + G - 1) // G
    NSLOT = NG * G

    x_d = nc.dram_tensor("x", [TOK, C], F32, kind="ExternalInput").ap()
    w_src = {}
    for n, nm in (("q", "wq"), ("k", "wk"), ("kp", "wkp"), ("v", "wv"),
                  ("o", "wo")):
        w_src[n] = nc.dram_tensor(nm, [C, C], F32, kind="ExternalInput").ap()
    alpha_d = nc.dram_tensor("alpha", [1, 1], F32, kind="ExternalInput").ap()
    wgr_d = nc.dram_tensor("wg_red", [GD, C], F32, kind="ExternalInput").ap()
    wg_d = nc.dram_tensor("wg", [NE, GD], F32, kind="ExternalInput").ap()
    w1_d = nc.dram_tensor("w1", [NE, C, HID], F32, kind="ExternalInput").ap()
    b1_d = nc.dram_tensor("b1", [NE, HID], F32, kind="ExternalInput").ap()
    w2_d = nc.dram_tensor("w2", [NE, HID, C], F32, kind="ExternalInput").ap()
    b2_d = nc.dram_tensor("b2", [NE, C], F32, kind="ExternalInput").ap()
    out_d = nc.dram_tensor("out", [TOK, C], F32, kind="ExternalOutput").ap()

    xn_feat_d = nc.dram_tensor("xn_feat_d", [CCH, 128, TOK], BF16,
                               kind="Internal").ap()
    xn_tok_d = nc.dram_tensor("xn_tok_d", [TOK, C], BF16, kind="Internal").ap()
    s_tok_d = nc.dram_tensor("s_tok_d", [TOK, C], F32, kind="Internal").ap()
    o_d = nc.dram_tensor("o_d", [128, CCH, T], BF16, kind="Internal").ap()
    g_slot_d = nc.dram_tensor("g_slot_d", [NSLOT, C], F32,
                              kind="Internal").ap()
    inv_d = nc.dram_tensor("inv_d", [NSLOT, 1], U32, kind="Internal").ap()
    score_tok_d = nc.dram_tensor("score_tok_d", [TOK, 1], F32,
                                 kind="Internal").ap()
    wt_d = {n: nc.dram_tensor(f"wt_{n}_d", [128, CCH, C], BF16,
                              kind="Internal").ap()
            for n in ("q", "k", "kp", "v", "o")}
    w1pre_d = nc.dram_tensor("w1pre_d", [NE * 128, CCH * HID], BF16,
                             kind="Internal").ap()
    w2pre_d = nc.dram_tensor("w2pre_d", [NE * 128, HJ * C], BF16,
                             kind="Internal").ap()
    b1pre_d = nc.dram_tensor("b1pre_d", [NE * 128, HJ], F32,
                             kind="Internal").ap()

    ident_f32 = nc.inline_tensor(np.eye(128, dtype=np.float32), name="idf").ap()
    ident_bf16 = nc.inline_tensor(np.eye(128, dtype=ml_dtypes.bfloat16),
                                  name="idb").ap()
    ones_row = nc.inline_tensor(np.ones((1, 128), dtype=ml_dtypes.bfloat16),
                                name="onesr").ap()
    ones_row_f = nc.inline_tensor(np.ones((1, 128), dtype=np.float32),
                                  name="onesrf").ap()
    u_np = np.triu(np.ones((128, 128), dtype=ml_dtypes.bfloat16), 1)
    u_tri = nc.inline_tensor(u_np, name="utri").ap()
    eiota_row = nc.inline_tensor(
        np.arange(NE, dtype=np.float32).reshape(1, NE), name="eiota").ap()
    gstart_row = nc.inline_tensor(
        (np.arange(NG, dtype=np.float32) * G).reshape(1, NG),
        name="gstart").ap()

    sch = Sched()
    es0 = contextlib.ExitStack()

    def SB(stack, name, shape, dtype):
        return stack.enter_context(nc.sbuf_tensor(name, shape, dtype))

    def PSUM(stack, name, shape, dtype=F32):
        return stack.enter_context(nc.psum_tensor(name, shape, dtype))

    wt_names = ["q", "k", "kp", "v", "o"]
    score_sb = SB(es0, "score", [128, NTT], F32)
    idxf_sb = SB(es0, "idxf", [128, NTT], F32)
    score_buf, idxf_buf = Buf(), Buf()
    rank_u = SB(es0, "rank_u", [128, NTT], U32)
    rank_u_buf = Buf()
    idxw_sb = SB(es0, "idxw_sb", [128, NG], U32)
    idxb2_sb = SB(es0, "idxb2_sb", [128, NG], U32)
    idxw_buf, idxb2_buf = Buf(), Buf()
    alpha_sb = SB(es0, "alpha_sb", [1, 2], F32)
    alpha_buf = Buf()
    id_f_sb = SB(es0, "id_f_sb", [128, 128], F32)
    ones_f_sb = SB(es0, "ones_f_sb", [1, 128], F32)
    eps_sb = SB(es0, "eps_sb", [128, 1], F32)
    id_b_sb = SB(es0, "id_b_sb", [128, 128], BF16)
    ones_sb = SB(es0, "ones_sb", [1, 128], BF16)
    wgr_t = SB(es0, "wgr_t", [128, CCH, GD], F32)
    wgu_t = SB(es0, "wgu_t", [GD, NE], F32)
    wgr_t_buf, wgu_t_buf = Buf(), Buf()

    esA = contextlib.ExitStack()
    WTs = SB(esA, "wt_stage", [128, CCH, C], BF16)
    WTs_buf = Buf()
    a_tmp = SB(esA, "a_tmp", [1, 1], F32)
    wg_sb = SB(esA, "wg_sb", [NE, GD], F32)
    wgr_sb = SB(esA, "wgr_sb", [GD, C], F32)
    sch.dma("sync", "dc", id_f_sb[:], ident_f32[:, :])
    sch.dma("sync", "dc", id_b_sb[:], ident_bf16[:, :])
    sch.dma("sync", "dc", ones_sb[:], ones_row[:, :])
    sch.dma("sync", "dc", ones_f_sb[:], ones_row_f[:, :])
    sch.dma("sync", "dc", a_tmp[:], alpha_d[:, :])
    sch.dma("sync", "dc", wg_sb[:], wg_d[:, :])
    sch.dma("sync", "dc", wgr_sb[:], wgr_d[:, :])

    tok_i32 = SB(esA, "tok_i32", [128, NTT], I32)
    toku_sb = SB(esA, "toku_sb", [128, NTT], U32)
    piota_f = SB(esA, "piota_f", [128, NG], F32)
    u_sb = SB(esA, "u_sb", [128, 128], BF16)
    eiota_sb = SB(esA, "eiota_sb", [1, NE], F32)
    gstart_sb = SB(esA, "gstart_sb", [1, NG], F32)
    sch.dma("sync", "dc", u_sb[:], u_tri[:, :])
    sch.dma("sync", "dc", eiota_sb[:], eiota_row[:, :])
    sch.dma("sync", "dc", gstart_sb[:], gstart_row[:, :])
    eps_tok = sch.add("vector", lambda eng: eng.memset(eps_sb[:], 1e-5),
                      inc="dve")
    sch.add("vector", lambda eng: eng.tensor_copy(alpha_sb[:, 0:1], a_tmp[:]),
            waits=[("dc", 0)], inc="dve")
    t = sch.add("vector",
                lambda eng: eng.tensor_scalar(alpha_sb[:, 1:2], a_tmp[:],
                                              -1.0, 1.0, ALU.mult, ALU.add),
                waits=[("dc", 0)], inc="dve")
    alpha_buf.wrote(t)

    t = sch.add("gpsimd",
                lambda eng: eng.iota(tok_i32[:], pattern=[[128, NTT]], base=0,
                                     channel_multiplier=1),
                inc="pool")
    toku_tok = sch.add("vector",
                       lambda eng: eng.tensor_copy(toku_sb[:], tok_i32[:]),
                       waits=[t], inc="dve")
    pio_i = SB(esA, "pio_i", [128, NG], I32)
    t = sch.add("gpsimd",
                lambda eng: eng.iota(pio_i[:], pattern=[[0, NG]], base=0,
                                     channel_multiplier=1),
                inc="pool")
    pio_tok = sch.add("vector",
                      lambda eng: eng.tensor_copy(piota_f[:], pio_i[:]),
                      waits=[t], inc="dve")

    wg_sq = SB(esA, "wg_sq", [NE, GD], F32)
    wg_n = SB(esA, "wg_n", [NE, 4], F32)
    wgu_sb = SB(esA, "wgu_sb", [NE, GD], F32)
    ps_small = PSUM(esA, "ps_small", [128, 512], F32)
    ps_small_buf = Buf()
    t = sch.add("scalar", lambda eng: eng.activation(
        wg_sq[:], wg_sb[:], AF.Square, accum_out=wg_n[:, 0:1]),
        waits=[("dc", 0)], inc="act")
    t = sch.add("scalar", lambda eng: eng.activation(
        wg_n[:, 2:3], wg_n[:, 0:1], AF.Sqrt), waits=[t], inc="act")
    t = sch.add("vector", lambda eng: eng.tensor_scalar(
        wg_n[:, 2:3], wg_n[:, 2:3], 1e-4, None, ALU.max),
        waits=[(t[0], t[1], True)], inc="dve")
    t = sch.add("vector", lambda eng: eng.reciprocal(wg_n[:, 3:4],
                                                     wg_n[:, 2:3]),
                waits=[t], inc="dve")
    t = sch.add("vector", lambda eng: eng.tensor_scalar(
        wgu_sb[:], wg_sb[:], wg_n[:, 3:4], None, ALU.mult),
        waits=[t], inc="dve")
    tmm = sch.add("tensor", lambda eng: eng.transpose(
        ps_small[0:GD, 0:NE], wgu_sb[:], id_f_sb[0:NE, 0:NE]),
        waits=[t, ("dc", 0)], inc="pe")
    ps_small_buf.wrote(tmm)
    t = sch.add("vector", lambda eng: eng.tensor_copy(
        wgu_t[:], ps_small[0:GD, 0:NE]), waits=[tmm], inc="dve")
    wgu_t_buf.wrote(t)
    ps_small_buf.read(t)
    for c in range(CCH):
        tmm = sch.add("tensor", lambda eng, c=c: eng.transpose(
            ps_small[0:128, 0:GD], wgr_sb[:, c * 128:(c + 1) * 128],
            id_f_sb[0:GD, 0:GD]),
            waits=ps_small_buf.write_waits() + [("dc", 0)], inc="pe")
        ps_small_buf.wrote(tmm)
        t = sch.add("vector", lambda eng, c=c: eng.tensor_copy(
            wgr_t[:, c, :], ps_small[0:128, 0:GD]), waits=[tmm], inc="dve")
        ps_small_buf.read(t)
        wgr_t_buf.w = t

    wtmp = [SB(esA, f"wtmp{s}", [128, C], F32) for s in range(2)]
    wtmp_buf = [Buf(), Buf()]
    ps_trf = [PSUM(esA, f"ps_trf{s}", [128, 128], F32) for s in range(4)]
    ps_trf_buf = [Buf() for _ in range(4)]
    for wi, n in enumerate(wt_names):
        for i in range(CCH):
            sl = (wi * CCH + i) % 2
            dtok = sch.dma("sync", f"dwt{sl}", wtmp[sl][:],
                           w_src[n][i * 128:(i + 1) * 128, :],
                           waits=wtmp_buf[sl].write_waits())
            wtmp_buf[sl].wrote(dtok)
            for c in range(CCH):
                psl = c % 4
                tmm = sch.add("tensor", lambda eng, sl=sl, c=c, psl=psl:
                              eng.transpose(ps_trf[psl][:],
                                            wtmp[sl][:, c * 128:(c + 1) * 128],
                                            id_f_sb[:]),
                              waits=ps_trf_buf[psl].write_waits() +
                              [(f"dwt{sl}", 0), ("dc", 0)], inc="pe")
                ps_trf_buf[psl].wrote(tmm)
                wtmp_buf[sl].read(tmm)
                w_ws = [tmm]
                if i == 0 and c == 0:
                    w_ws += WTs_buf.write_waits()
                tcp = sch.add("vector", lambda eng, n=n, c=c, i=i, psl=psl:
                              eng.tensor_copy(
                                  WTs[:, c, i * 128:(i + 1) * 128],
                                  ps_trf[psl][:]), waits=w_ws, inc="dve")
                ps_trf_buf[psl].read(tcp)
                WTs_buf.w = tcp
        stw = sch.dma("sync", "dst", wt_d[n].rearrange("p c i -> p (c i)"),
                      WTs[:].rearrange("p c i -> p (c i)"),
                      waits=[WTs_buf.w])
        WTs_buf.read(stw)

    b1_raw = SB(esA, "b1_raw", [HJ, 128], F32)
    b1_raw_buf = Buf()
    b1_t = SB(esA, "b1_t", [128, HJ], F32)
    b1_t_buf = Buf()
    ps_wtp = PSUM(esA, "ps_wtp", [128, 32], F32)
    ps_wtp_buf = Buf()
    for e in range(NE):
        ld = sch.dma("sync", "dx0", b1_raw[:],
                     b1_d[e].rearrange("(s p) -> s p", p=128),
                     waits=b1_raw_buf.write_waits())
        b1_raw_buf.wrote(ld)
        tm_ = sch.add("tensor", lambda eng: eng.transpose(
            ps_wtp[:, 0:HJ], b1_raw[:], id_f_sb[0:HJ, 0:HJ]),
            waits=[("dx0", 0), ("dc", 0)] + ps_wtp_buf.write_waits(),
            inc="pe")
        ps_wtp_buf.wrote(tm_)
        b1_raw_buf.read(tm_)
        tc_ = sch.add("vector", lambda eng: eng.tensor_copy(
            b1_t[:], ps_wtp[:, 0:HJ]),
            waits=[tm_] + b1_t_buf.write_waits(), inc="dve")
        ps_wtp_buf.read(tc_)
        b1_t_buf.wrote(tc_)
        st = sch.dma("sync", "dst", b1pre_d[e * 128:(e + 1) * 128, :],
                     b1_t[:], waits=[tc_])
        b1_t_buf.read(st)
    eiota_bc = SB(esA, "eiota_bc", [128, NE], F32)
    onescol = SB(esA, "onescol", [128, 1], BF16)
    oc_tok = sch.add("vector", lambda eng: eng.memset(onescol[:], 1.0),
                     inc="dve")
    tm_ = sch.add("tensor", lambda eng: eng.matmul(
        ps_wtp[:, 0:NE], ones_f_sb[0:1, :], eiota_sb[:],
        start=True, stop=True),
        waits=[("dc", 0)] + ps_wtp_buf.write_waits(), inc="pe")
    ps_wtp_buf.wrote(tm_)
    ebc_tok = sch.add("vector", lambda eng: eng.tensor_copy(
        eiota_bc[:], ps_wtp[:, 0:NE]), waits=[tm_], inc="dve")
    ps_wtp_buf.read(ebc_tok)
    oh_sb = SB(esA, "oh_sb", [128, NTT, NE], BF16)
    oh_buf = Buf()
    wtp_sb = SB(esA, "wtp_sb", [128, NTT, NE], F32)
    wtp_buf = Buf()
    hist_sb = SB(esA, "hist_sb", [1, NTT, NE], F32)
    hist_buf = Buf()

    xn_feat_sb = SB(esA, "xn_feat", [128, CCH, TOK], BF16)
    xn_feat_buf = Buf()
    NSL = 4
    x_t = [SB(esA, f"x_t{s}", [128, C], F32) for s in range(NSL)]
    x_t_buf = [Buf() for _ in range(NSL)]
    ln_sc = [SB(esA, f"ln_sc{s}", [128, 16], F32) for s in range(NSL)]
    sq_t1 = SB(esA, "sq_t", [128, C], F32)
    sq_t = [sq_t1] * NSL
    sq_common = Buf()
    sq_t_buf = [sq_common] * NSL
    xn_f = [SB(esA, f"xn_f{s}", [128, C], F32) for s in range(NSL)]
    xn_f_buf = [Buf() for _ in range(NSL)]
    xn_b = [SB(esA, f"xn_b{s}", [128, C], BF16) for s in range(NSL)]
    xn_b_buf = [Buf() for _ in range(NSL)]
    xnT_f = [SB(esA, f"xnT_f{s}", [128, CCH, 128], F32) for s in range(NSL)]
    xnT_f_buf = [Buf() for _ in range(NSL)]
    xr_sb = [SB(esA, f"xr_sb{s}", [GD, 128], F32) for s in range(NSL)]
    xr_buf = [Buf() for _ in range(NSL)]
    l_sb = [SB(esA, f"l_sb{s}", [128, NE], F32) for s in range(NSL)]
    l_buf = [Buf() for _ in range(NSL)]
    g_sc = [SB(esA, f"g_sc{s}", [128, 16], F32) for s in range(NSL)]
    gmax_buf = [Buf() for _ in range(NSL)]
    gtop_buf = [Buf() for _ in range(NSL)]
    gsum_buf = [Buf() for _ in range(NSL)]
    g_e8 = [SB(esA, f"g_e8{s}", [128, NE], F32) for s in range(NSL)]
    g_u8 = [SB(esA, f"g_u8{s}", [128, 8], U32) for s in range(NSL)]
    ps_xr = PSUM(esA, "ps_xr", [GD, 128], F32)
    ps_xr_buf = Buf()
    ps_lg = PSUM(esA, "ps_lg", [128, NE], F32)
    ps_lg_buf = Buf()

    for tt in range(NTT):
        sl = tt % NSL
        dtok = sch.dma("sync", f"dx{sl}", x_t[sl][:],
                       x_d[tt * 128:(tt + 1) * 128, :],
                       waits=x_t_buf[sl].write_waits())
        x_t_buf[sl].wrote(dtok)
        NBS = C // 512
        tbs = []
        for i in range(NBS):
            tbi = sch.add("vector", lambda eng, sl=sl, i=i: eng.bn_stats(
                ln_sc[sl][:, i * 6:(i + 1) * 6],
                x_t[sl][:, i * 512:(i + 1) * 512]),
                waits=[(f"dx{sl}", 0)], inc="dve")
            x_t_buf[sl].read(tbi)
            tbs.append(tbi)
        tb3 = sch.add("vector", lambda eng, sl=sl: eng.bn_aggr(
            ln_sc[sl][:, 12:14], ln_sc[sl][:, 0:6 * NBS]),
            waits=tbs, inc="dve")
        t7 = sch.add("scalar", lambda eng, sl=sl: eng.activation(
            ln_sc[sl][:, 6:7], ln_sc[sl][:, 13:14], AF.Ln,
            bias=eps_sb[:, 0:1]),
            waits=[(tb3[0], tb3[1], False), eps_tok], inc="act")
        t8 = sch.add("scalar", lambda eng, sl=sl: eng.activation(
            ln_sc[sl][:, 7:8], ln_sc[sl][:, 6:7], AF.Exp, scale=-0.5),
            waits=[(t7[0], t7[1], True)], inc="act")
        t9 = sch.add("vector", lambda eng, sl=sl: eng.scalar_tensor_tensor(
            ln_sc[sl][:, 0:1], ln_sc[sl][:, 12:13], -1.0, ln_sc[sl][:, 7:8],
            ALU.mult, ALU.mult), waits=[(t8[0], t8[1], False), tb3],
            inc="dve")
        t10 = sch.add("scalar", lambda eng, sl=sl: eng.activation(
            xn_f[sl][:], x_t[sl][:], AF.Identity, bias=ln_sc[sl][:, 0:1],
            scale=ln_sc[sl][:, 7:8]),
            waits=[(t9[0], t9[1], True), (f"dx{sl}", 0)] +
            xn_f_buf[sl].write_waits(), inc="act")
        xn_f_buf[sl].wrote(t10)
        x_t_buf[sl].read(t10)
        t11 = sch.add("vector", lambda eng, sl=sl: eng.tensor_copy(
            xn_b[sl][:], xn_f[sl][:]),
            waits=[(t10[0], t10[1], False)] + xn_b_buf[sl].write_waits(),
            inc="dve")
        xn_b_buf[sl].wrote(t11)
        xn_f_buf[sl].read(t11)

        for c in range(CCH):
            psl = c % 4
            w_ws = [t10] + ps_trf_buf[psl].write_waits()
            tmm = sch.add("tensor", lambda eng, sl=sl, c=c, psl=psl:
                          eng.transpose(ps_trf[psl][:],
                                        xn_f[sl][:, c * 128:(c + 1) * 128],
                                        id_f_sb[:]),
                          waits=w_ws, inc="pe")
            ps_trf_buf[psl].wrote(tmm)
            xn_f_buf[sl].read(tmm)
            w_ws = [tmm] + (xnT_f_buf[sl].write_waits() if c == 0 else [])
            tcv = sch.add("vector", lambda eng, sl=sl, c=c, psl=psl:
                          eng.tensor_copy(xnT_f[sl][:, c, :], ps_trf[psl][:]),
                          waits=w_ws, inc="dve")
            ps_trf_buf[psl].read(tcv)
            if c == 0:
                xnT_f_buf[sl].wrote(tcv)
            else:
                xnT_f_buf[sl].w = tcv
            tcb = sch.add("vector", lambda eng, tt=tt, c=c, psl=psl:
                          eng.tensor_copy(
                              xn_feat_sb[:, c, tt * 128:(tt + 1) * 128],
                              ps_trf[psl][:]), waits=[tmm], inc="dve")
            ps_trf_buf[psl].read(tcb)
            xn_feat_buf.w = tcb

        last = None
        for c in range(CCH):
            w_ws = [xnT_f_buf[sl].w, wgr_t_buf.w]
            if c == 0:
                w_ws += ps_xr_buf.write_waits()
            last = sch.add("tensor", lambda eng, sl=sl, c=c: eng.matmul(
                ps_xr[:], wgr_t[:, c, :], xnT_f[sl][:, c, :],
                start=(c == 0), stop=(c == CCH - 1)),
                waits=w_ws, inc="pe" if c == CCH - 1 else None)
        ps_xr_buf.wrote(last)
        xnT_f_buf[sl].read(last)
        t = sch.add("vector", lambda eng, sl=sl: eng.tensor_copy(
            xr_sb[sl][:], ps_xr[:]),
            waits=[last] + xr_buf[sl].write_waits(), inc="dve")
        xr_buf[sl].wrote(t)
        ps_xr_buf.read(t)
        tl = sch.add("tensor", lambda eng, sl=sl: eng.matmul(
            ps_lg[:], xr_sb[sl][:], wgu_t[:], start=True, stop=True),
            waits=[t, wgu_t_buf.w] + ps_lg_buf.write_waits(), inc="pe")
        ps_lg_buf.wrote(tl)
        xr_buf[sl].read(tl)
        tle = sch.add("vector", lambda eng, sl=sl: eng.tensor_copy(
            l_sb[sl][:], ps_lg[:]),
            waits=[tl] + l_buf[sl].write_waits(), inc="dve")
        l_buf[sl].wrote(tle)
        ps_lg_buf.read(tle)
        tm = sch.add("vector", lambda eng, sl=sl: eng.tensor_reduce(
            g_sc[sl][:, 0:1], l_sb[sl][:], AX.X, ALU.max, negate=True),
            waits=[tle] + gmax_buf[sl].write_waits(), inc="dve")
        gmax_buf[sl].wrote(tm)
        l_buf[sl].read(tm)
        t8m = sch.add("vector", lambda eng, sl=sl: eng.max(
            g_sc[sl][:, 8:16], l_sb[sl][:]),
            waits=[tle] + gtop_buf[sl].write_waits(), inc="dve")
        gtop_buf[sl].wrote(t8m)
        l_buf[sl].read(t8m)
        tix = sch.add("vector", lambda eng, sl=sl: eng.max_index(
            g_u8[sl][:], g_sc[sl][:, 8:16], l_sb[sl][:]),
            waits=[(t8m[0], t8m[1], True)], inc="dve")
        gtop_buf[sl].read(tix)
        l_buf[sl].read(tix)
        te = sch.add("scalar", lambda eng, sl=sl: eng.activation(
            g_e8[sl][:], l_sb[sl][:], AF.Exp, bias=g_sc[sl][:, 0:1],
            accum_out=g_sc[sl][:, 1:2]),
            waits=[tm, tle] + gsum_buf[sl].write_waits(), inc="act")
        gsum_buf[sl].wrote(te)
        gmax_buf[sl].read(te)
        l_buf[sl].read(te)
        ts = sch.add("vector", lambda eng, sl=sl, tt=tt: eng.reciprocal(
            score_sb[:, tt:tt + 1], g_sc[sl][:, 1:2]),
            waits=[(te[0], te[1], True)], inc="dve")
        gsum_buf[sl].read(ts)
        score_buf.w = ts
        tif = sch.add("vector", lambda eng, sl=sl, tt=tt: eng.tensor_copy(
            idxf_sb[:, tt:tt + 1], g_u8[sl][:, 0:1]),
            waits=[(tix[0], tix[1], True)], inc="dve")
        idxf_buf.w = tif

        toh = sch.add("vector", lambda eng, tt=tt: eng.tensor_scalar(
            oh_sb[:, tt, :], eiota_bc[:], idxf_sb[:, tt:tt + 1], None,
            ALU.is_equal), waits=[tif, ebc_tok] + oh_buf.readers,
            inc="dve")
        oh_buf.w = toh
        thm = sch.add("tensor", lambda eng, tt=tt: eng.matmul(
            ps_small[0:1, 0:NE], onescol[:], oh_sb[:, tt, :],
            start=True, stop=True),
            waits=[toh, oc_tok] + ps_small_buf.write_waits(), inc="pe")
        ps_small_buf.wrote(thm)
        oh_buf.readers.append(thm)
        the = sch.add("vector", lambda eng, tt=tt: eng.tensor_copy(
            hist_sb[:, tt, :], ps_small[0:1, 0:NE]),
            waits=[thm], inc="dve")
        ps_small_buf.read(the)
        hist_buf.w = the
        twm = sch.add("tensor", lambda eng, tt=tt: eng.matmul(
            ps_wtp[:, 0:NE], u_sb[:], oh_sb[:, tt, :],
            start=True, stop=True),
            waits=[toh, ("dc", 0)] + ps_wtp_buf.write_waits(), inc="pe")
        ps_wtp_buf.wrote(twm)
        oh_buf.readers.append(twm)
        twe = sch.add("vector", lambda eng, tt=tt: eng.tensor_copy(
            wtp_sb[:, tt, :], ps_wtp[:, 0:NE]),
            waits=[twm], inc="dve")
        ps_wtp_buf.read(twe)
        wtp_buf.w = twe
        stx = sch.dma("sync", "dst", xn_tok_d[tt * 128:(tt + 1) * 128, :],
                      xn_b[sl][:], waits=[t11])
        xn_b_buf[sl].read(stx)

    xn_store = sch.dma("sync", "dxn", xn_feat_d.rearrange("c p t -> p c t"),
                       xn_feat_sb[:], waits=[xn_feat_buf.w])
    xn_feat_buf.read(xn_store)

    cum_sb = SB(esA, "cum_sb", [1, NTT + 1, NE], F32)
    scr8 = SB(esA, "scr8", [128, NE], F32)
    scr8b = SB(esA, "scr8b", [128, NE], F32)
    rank_f = SB(esA, "rank_f", [128, 1], F32)
    cnt_u = SB(esA, "cnt_u", [1, NE], U32)
    po_u = SB(esA, "po_u", [1, NE + 1], U32)
    po_f = SB(esA, "po_f", [1, NE + 1], F32)
    grpe_f = SB(esA, "grpe_f", [1, NG], F32)
    sct_sb = SB(esA, "sct_sb", [NTT, 128], F32)
    tcum = sch.add("vector", lambda eng: eng.memset(cum_sb[:, 0, :], 0.0),
                   inc="dve")
    for tt in range(NTT):
        tcum = sch.add("vector", lambda eng, tt=tt: eng.tensor_add(
            cum_sb[:, tt + 1, :], cum_sb[:, tt, :], hist_sb[:, tt, :]),
            waits=[tcum, hist_buf.w], inc="dve")
    t1 = sch.add("vector", lambda eng: eng.tensor_copy(
        cnt_u[:], cum_sb[:, NTT, :]), waits=[tcum], inc="dve")
    t1 = sch.add("vector", lambda eng: eng.tensor_scalar(
        cnt_u[:], cnt_u[:], G - 1, None, ALU.add), waits=[t1], inc="dve")
    t1 = sch.add("vector", lambda eng: eng.tensor_scalar(
        cnt_u[:], cnt_u[:], 8, None, ALU.logical_shift_right),
        waits=[t1], inc="dve")
    t1 = sch.add("vector", lambda eng: eng.tensor_scalar(
        cnt_u[:], cnt_u[:], 8, None, ALU.logical_shift_left),
        waits=[t1], inc="dve")
    tpo = sch.add("vector", lambda eng: eng.memset(po_u[:, 0:1], 0),
                  inc="dve")
    for e in range(NE):
        tpo = sch.add("vector", lambda eng, e=e: eng.tensor_add(
            po_u[:, e + 1:e + 2], po_u[:, e:e + 1], cnt_u[:, e:e + 1]),
            waits=[tpo, t1], inc="dve")
    tpf = sch.add("vector", lambda eng: eng.tensor_copy(po_f[:], po_u[:]),
                  waits=[tpo], inc="dve")
    tg = sch.add("vector", lambda eng: eng.memset(grpe_f[:], 0.0),
                 inc="dve")
    for e in range(NE):
        tg = sch.add("vector", lambda eng, e=e: eng.scalar_tensor_tensor(
            grpe_f[:], gstart_sb[:], po_f[0:1, e + 1:e + 2], grpe_f[:],
            ALU.is_ge, ALU.add), waits=[tg, tpf, ("dc", 0)], inc="dve")
    tbm = sch.add("tensor", lambda eng: eng.matmul(
        ps_wtp[:, 0:NG], ones_f_sb[0:1, :], grpe_f[:],
        start=True, stop=True),
        waits=[tg, ("dc", 0)] + ps_wtp_buf.write_waits(), inc="pe")
    ps_wtp_buf.wrote(tbm)
    tie = sch.add("vector", lambda eng: eng.tensor_copy(
        idxb2_sb[:], ps_wtp[:, 0:NG]), waits=[tbm], inc="dve")
    idxb2_buf.wrote(tie)
    ps_wtp_buf.read(tie)
    tiw = sch.add("vector", lambda eng: eng.scalar_tensor_tensor(
        piota_f[:], ps_wtp[:, 0:NG], 128.0, piota_f[:], ALU.mult, ALU.add),
        waits=[tbm, pio_tok], inc="dve")
    ps_wtp_buf.read(tiw)
    tiw = sch.add("vector", lambda eng: eng.tensor_copy(
        idxw_sb[:], piota_f[:]), waits=[tiw], inc="dve")
    idxw_buf.wrote(tiw)
    tcb0 = None
    for tt in range(NTT):
        tcb0 = sch.add("vector", lambda eng, tt=tt: eng.tensor_add(
            cum_sb[:, tt, :], cum_sb[:, tt, :], po_f[0:1, 0:NE]),
            waits=[tpf, tcum] + ([] if tcb0 is None else [tcb0]),
            inc="dve")
    for tt in range(NTT):
        tbm = sch.add("tensor", lambda eng, tt=tt: eng.matmul(
            ps_wtp[:, 0:NE], ones_f_sb[0:1, :], cum_sb[:, tt, :],
            start=True, stop=True),
            waits=[tcb0] + ps_wtp_buf.write_waits(), inc="pe")
        ps_wtp_buf.wrote(tbm)
        tr1 = sch.add("vector", lambda eng, tt=tt: eng.tensor_add(
            scr8[:], wtp_sb[:, tt, :], ps_wtp[:, 0:NE]),
            waits=[tbm, wtp_buf.w], inc="dve")
        ps_wtp_buf.read(tr1)
        tr2 = sch.add("vector", lambda eng, tt=tt: eng.tensor_tensor(
            scr8b[:], scr8[:], oh_sb[:, tt, :], ALU.mult),
            waits=[tr1, oh_buf.w], inc="dve")
        oh_buf.readers.append(tr2)
        tr3 = sch.add("vector", lambda eng: eng.tensor_reduce(
            rank_f[:], scr8b[:], AX.X, ALU.add), waits=[tr2], inc="dve")
        tr4 = sch.add("vector", lambda eng, tt=tt: eng.tensor_copy(
            rank_u[:, tt:tt + 1], rank_f[:]),
            waits=[(tr3[0], tr3[1], True)], inc="dve")
        rank_u_buf.w = tr4
        tsc = sch.add("gpsimd", lambda eng, tt=tt: eng.indirect_dma_start(
            out=inv_d[:, :],
            out_offset=bass.IndirectOffsetOnAxis(
                ap=rank_u[:, tt:tt + 1], axis=0),
            in_=toku_sb[:, tt:tt + 1], in_offset=None),
            waits=[(tr4[0], tr4[1], False), toku_tok, ("dsc", 0)],
            inc="dsc", amt=16)
        rank_u_buf.readers.append(tsc)
    tsm = sch.add("tensor", lambda eng: eng.transpose(
        ps_small[0:NTT, 0:128], score_sb[:], id_f_sb[:]),
        waits=[score_buf.w, ("dc", 0)] + ps_small_buf.write_waits(),
        inc="pe")
    ps_small_buf.wrote(tsm)
    score_buf.read(tsm)
    tse = sch.add("vector", lambda eng: eng.tensor_copy(
        sct_sb[:], ps_small[0:NTT, 0:128]), waits=[tsm], inc="dve")
    ps_small_buf.read(tse)
    sts = sch.dma("sync", "dst",
                  score_tok_d.rearrange("(t p) o -> t (p o)", p=128),
                  sct_sb[:], waits=[tse])

    NIQpre = C // 256
    pcnt = 0
    for e in range(NE):
        sl = pcnt % 2
        pcnt += 1
        sch.dma("gpsimd", f"dp{sl}",
                w1pre_d[e * 128:(e + 1) * 128, :].rearrange(
                    "p (c j) -> p c j", c=CCH),
                w1_d[e].rearrange("(c p) j -> p c j", p=128))
    for e in range(NE):
        for iQ in range(NIQpre):
            sl = pcnt % 2
            pcnt += 1
            sch.dma("gpsimd", f"dp{sl}",
                    w2pre_d[e * 128:(e + 1) * 128,
                            iQ * HJ * 256:(iQ + 1) * HJ * 256].rearrange(
                        "p (j i) -> p j i", j=HJ),
                    w2_d[e].rearrange("(j p) i -> p j i", p=128)
                    [:, :, iQ * 256:(iQ + 1) * 256])

    yield "A", sch, es0, esA, None
    sch.barrier(exclude=("dp0", "dp1"))

    esB = contextlib.ExitStack()
    NIQ = C // 256
    PJW = 512 if QT >= 512 else QT
    wts = {n: [SB(esB, f"wts_{n}{i}", [128, CCH, 128], BF16)
               for i in range(2)] for n in ("q", "k", "kp", "v")}
    wts_buf = {n: [Buf(), Buf()] for n in ("q", "k", "kp", "v")}
    wto_h = SB(esB, "wto_h", [128, CCH, PJW], BF16)
    wto_buf = Buf()
    xnb_sb = SB(esB, "xnb_sb", [128, CCH, T], BF16)
    xnb_buf = Buf()
    q_sb = SB(esB, "q_sb", [128, T], BF16)
    k_sb = SB(esB, "k_sb", [128, T], BF16)
    kp_sb = SB(esB, "kp_sb", [128, T], BF16)
    v_sb = SB(esB, "v_sb", [128, KC, 130], BF16)
    qkv_buf = {n: Buf() for n in ("q", "k", "kp", "v")}
    aT = [SB(esB, f"aT{s}", [128, 2, QT], BF16) for s in range(2)]
    aT_buf = [Buf(), Buf()]
    nrow_bf = SB(esB, "nrow_bf", [1, QT], BF16)
    nrow_buf = Buf()
    alpharow = SB(esB, "alpharow", [1, 2, 64], BF16)
    alpharow_buf = Buf()
    rc_sb = SB(esB, "rc_sb", [1, QT], F32)
    rc_buf = Buf()
    rc2_sb = SB(esB, "rc2_sb", [1, QT], F32)
    rc2_buf = Buf()
    nbc_sb = SB(esB, "nbc_sb", [128, QT], BF16)
    nbc_buf = Buf()
    otmp_sb = SB(esB, "otmp_sb", [128, QT], BF16)
    otmp_buf = Buf()
    o_stage = [SB(esB, f"o_stage{s}", [128, QT], BF16) for s in range(2)]
    o_stage_buf = [Buf(), Buf()]
    osl_sb = [SB(esB, f"osl_sb{s}", [128, CCH, 128], BF16) for s in range(2)]
    osl_buf = [Buf(), Buf()]
    s_stage = SB(esB, "s_stage", [128, C], F32)
    s_stage_buf = Buf()
    w1g = SB(esB, "w1g", [128, CCH, HID], BF16)
    w1g_buf = Buf()
    w2g = SB(esB, "w2g", [128, HJ, 256], BF16)
    w2g_buf = Buf()
    b1g = SB(esB, "b1g", [128, HJ], F32)
    b1g_buf = Buf()
    b2g = SB(esB, "b2g", [128, C], BF16)
    b2g_buf = Buf()
    invg = [SB(esB, f"invg{i}", [128, 1], U32) for i in range(2)]
    invg_buf = [Buf(), Buf()]
    xg_f = [SB(esB, f"xg_f{i}", [128, C], F32) for i in range(2)]
    xg_buf = [Buf(), Buf()]
    scg = [SB(esB, f"scg{i}", [128, 1], F32) for i in range(2)]
    scg_buf = [Buf(), Buf()]
    xgT = SB(esB, "xgT", [128, CCH, G], BF16)
    xgT_buf = Buf()
    h_sb = SB(esB, "h_sb", [128, HJ, G], BF16)
    h_buf = Buf()
    gs_sb = SB(esB, "gs_sb", [128, 256], F32)
    gs_buf = Buf()
    sc2 = [PSUM(esB, f"sc2_{i}", [128, 2, QT], F32) for i in range(2)]
    sc2_buf = [Buf(), Buf()]
    ps_o = [PSUM(esB, f"ps_o{i}", [65, QT], F32) for i in range(2)]
    ps_o_buf = [Buf(), Buf()]
    ps_pj = [PSUM(esB, f"ps_pj{i}", [128, PJW], F32) for i in range(2)]
    ps_pj_buf = [Buf(), Buf()]
    ps_h = ps_pj[1]
    ps_h_buf = ps_pj_buf[1]

    bnd_w = nc.gpsimd.alloc_register("bnd_w")
    bnd_e = nc.gpsimd.alloc_register("bnd_e")
    bnd_t = nc.gpsimd.alloc_register("bnd_t")
    tbw = sch.add("gpsimd", lambda eng: eng.reg_mov(bnd_w, NE * 128 - 1),
                  inc="pool")
    tbe = sch.add("gpsimd", lambda eng: eng.reg_mov(bnd_e, NE - 1),
                  inc="pool")
    tbt = sch.add("gpsimd", lambda eng: eng.reg_mov(bnd_t, TOK - 1),
                  inc="pool")

    def pj(i):
        return ps_pj[i][:, :]

    for _m in range(2):
        t_ar = sch.add("vector", lambda eng, _m=_m: eng.tensor_scalar(
            alpharow[:, _m, :], ones_sb[0:1, 0:64],
            alpha_sb[0:1, _m:_m + 1], None, ALU.mult),
            waits=[alpha_buf.w, ("dc", 0)], inc="dve")
        alpharow_buf.w = t_ar

    def moe_stream():
        for g in range(NG):
            for tm in range(2):
                ld = sch.dma("sync", f"dn{tm}", invg[tm][:],
                             inv_d[g * G + tm * 128:
                                   g * G + (tm + 1) * 128, :],
                             waits=invg_buf[tm].write_waits())
                invg_buf[tm].wrote(ld)
                gx = sch.add("gpsimd", lambda eng, tm=tm:
                             eng.indirect_dma_start(
                                 out=xg_f[tm][:], out_offset=None,
                                 in_=xn_tok_d[:, :],
                                 in_offset=bass.IndirectOffsetOnAxis(
                                     ap=invg[tm][:, 0:1], axis=0),
                                 bounds_check=bnd_t, oob_is_err=False),
                             waits=[(f"dn{tm}", 0), tbt] +
                             xg_buf[tm].write_waits(),
                             inc=f"dp{tm}", amt=16)
                xg_buf[tm].wrote(gx)
                invg_buf[tm].read(gx)
                gs_ = sch.add("gpsimd", lambda eng, tm=tm:
                              eng.indirect_dma_start(
                                  out=scg[tm][:], out_offset=None,
                                  in_=score_tok_d[:, :],
                                  in_offset=bass.IndirectOffsetOnAxis(
                                      ap=invg[tm][:, 0:1], axis=0),
                                  bounds_check=bnd_t, oob_is_err=False),
                              waits=[(f"dn{tm}", 0), tbt] +
                              scg_buf[tm].write_waits(),
                              inc=f"dp{tm}", amt=16)
                scg_buf[tm].wrote(gs_)
                invg_buf[tm].read(gs_)
            gw1 = sch.add("gpsimd", lambda eng, g=g: eng.indirect_dma_start(
                out=w1g[:].rearrange("p c j -> p (c j)"), out_offset=None,
                in_=w1pre_d[:, :],
                in_offset=bass.IndirectOffsetOnAxis(
                    ap=idxw_sb[:, g:g + 1], axis=0),
                bounds_check=bnd_w, oob_is_err=False),
                waits=[idxw_buf.w, tbw, ("dp0", 0), ("dp1", 0)] +
                w1g_buf.write_waits(),
                inc="dw0", amt=16)
            w1g_buf.wrote(gw1)
            idxw_buf.read(gw1)
            gb1 = sch.add("gpsimd", lambda eng, g=g: eng.indirect_dma_start(
                out=b1g[:], out_offset=None,
                in_=b1pre_d[:, :],
                in_offset=bass.IndirectOffsetOnAxis(
                    ap=idxw_sb[:, g:g + 1], axis=0),
                bounds_check=bnd_w, oob_is_err=False),
                waits=[idxw_buf.w, tbw] + b1g_buf.write_waits(),
                inc="dw2", amt=16)
            b1g_buf.wrote(gb1)
            idxw_buf.read(gb1)
            gb2 = sch.add("gpsimd", lambda eng, g=g: eng.indirect_dma_start(
                out=b2g[:], out_offset=None,
                in_=b2_d[:, :],
                in_offset=bass.IndirectOffsetOnAxis(
                    ap=idxb2_sb[:, g:g + 1], axis=0),
                bounds_check=bnd_e, oob_is_err=False),
                waits=[idxb2_buf.w, tbe] + b2g_buf.write_waits(),
                inc="dw2", amt=16)
            b2g_buf.wrote(gb2)
            idxb2_buf.read(gb2)
            yield
            for tm in range(2):
                for c in range(CCH):
                    tb = (tm * CCH + c) % 2
                    tmm = sch.add("tensor", lambda eng, tm=tm, c=c, tb=tb:
                                  eng.transpose(
                                      ps_pj[tb][:, 0:128],
                                      xg_f[tm][:, c * 128:(c + 1) * 128],
                                      id_f_sb[:]),
                                  waits=[xg_buf[tm].w, ("dc", 0)] +
                                  ps_pj_buf[tb].write_waits(), inc="pe")
                    ps_pj_buf[tb].wrote(tmm)
                    xg_buf[tm].read(tmm)
                    w_ws = [tmm]
                    if tm == 0 and c == 0:
                        w_ws += xgT_buf.write_waits()
                    tcv = sch.add("vector", lambda eng, tm=tm, c=c, tb=tb:
                                  eng.tensor_copy(
                                      xgT[:, c, tm * 128:(tm + 1) * 128],
                                      ps_pj[tb][:, 0:128]),
                                  waits=w_ws, inc="dve")
                    ps_pj_buf[tb].read(tcv)
                    if tm == 0 and c == 0:
                        xgT_buf.wrote(tcv)
                    else:
                        xgT_buf.w = tcv
            yield
            for j in range(HJ):
                hb = j % 2
                last = None
                for c in range(CCH):
                    w_ws = [w1g_buf.w, xgT_buf.w]
                    if c == 0:
                        w_ws += ps_pj_buf[hb].write_waits()
                    last = sch.add("tensor", lambda eng, j=j, c=c, hb=hb:
                                   eng.matmul(
                                       ps_pj[hb][:, 0:G],
                                       w1g[:, c, j * 128:(j + 1) * 128],
                                       xgT[:, c, :],
                                       start=(c == 0), stop=(c == CCH - 1)),
                                   waits=w_ws,
                                   inc="pe" if c == CCH - 1 else None)
                ps_pj_buf[hb].wrote(last)
                w1g_buf.read(last)
                xgT_buf.read(last)
                w_ws = [last, b1g_buf.w] + (h_buf.write_waits() if j == 0
                                            else [])
                tg_ = sch.add("scalar", lambda eng, j=j, hb=hb:
                              eng.activation(h_sb[:, j, :],
                                             ps_pj[hb][:, 0:G],
                                             AF.Gelu, bias=b1g[:, j:j + 1]),
                              waits=w_ws, inc="act")
                ps_pj_buf[hb].read(tg_)
                b1g_buf.read(tg_)
                if j == 0:
                    h_buf.wrote(tg_)
                else:
                    h_buf.w = tg_
                if j % 8 == 7:
                    yield
            for iQ in range(NIQ):
                gw2 = sch.add("gpsimd", lambda eng, g=g, iQ=iQ:
                              eng.indirect_dma_start(
                                  out=w2g[:].rearrange("p j i -> p (j i)"),
                                  out_offset=None,
                                  in_=w2pre_d[:, :],
                                  in_offset=bass.IndirectOffsetOnAxis(
                                      ap=idxw_sb[:, g:g + 1], axis=0),
                                  element_offset=iQ * HJ * 256,
                                  bounds_check=bnd_w, oob_is_err=False),
                              waits=[idxw_buf.w, tbw, ("dp0", 0),
                                     ("dp1", 0)] +
                              w2g_buf.write_waits(),
                              inc="dw1", amt=16)
                w2g_buf.wrote(gw2)
                idxw_buf.read(gw2)
                for tm in range(2):
                    last = None
                    for j in range(HJ):
                        w_ws = [h_buf.w, w2g_buf.w]
                        if j == 0:
                            w_ws += ps_pj_buf[0].write_waits()
                        last = sch.add("tensor", lambda eng, j=j, tm=tm:
                                       eng.matmul(
                                           ps_pj[0][:, 0:256],
                                           h_sb[:, j,
                                                tm * 128:(tm + 1) * 128],
                                           w2g[:, j, :],
                                           start=(j == 0), stop=False),
                                       waits=w_ws, inc=None)
                    last = sch.add("tensor", lambda eng, iQ=iQ:
                                   eng.matmul(
                                       ps_pj[0][:, 0:256], ones_sb[:],
                                       b2g[0:1, iQ * 256:(iQ + 1) * 256],
                                       start=False, stop=True),
                                   waits=[b2g_buf.w, ("dc", 0)], inc="pe")
                    ps_pj_buf[0].wrote(last)
                    h_buf.read(last)
                    w2g_buf.read(last)
                    b2g_buf.read(last)
                    tev = sch.add("scalar", lambda eng, tm=tm:
                                  eng.activation(
                                      gs_sb[:], ps_pj[0][:, 0:256], AF.Silu,
                                      scale=scg[tm][:, 0:1]),
                                  waits=[last, scg_buf[tm].w] +
                                  gs_buf.write_waits(), inc="act")
                    gs_buf.wrote(tev)
                    ps_pj_buf[0].read(tev)
                    scg_buf[tm].read(tev)
                    st = sch.dma("sync", "dst",
                                 g_slot_d[g * G + tm * 128:
                                          g * G + (tm + 1) * 128,
                                          iQ * 256:(iQ + 1) * 256],
                                 gs_sb[:], waits=[tev])
                    gs_buf.read(st)
                yield

    moe_gen = moe_stream()
    moe_done = [False]
    n_yields = NG * (2 + (HJ + 7) // 8 + NIQ) + 2
    n_a_sites = B * HC * NQT * 3 + B * (C // PJW) * (T // 128)
    a_every = max(1, ((n_a_sites * 3) // 4) // n_yields)
    n_b_sites = B * HC * NQT * 2 * KC
    b_every = max(1, (n_b_sites * 6) // n_yields)
    actr = [0]
    bctr = [0]

    a_delay = n_a_sites // 4
    b_delay = n_b_sites // 4

    def pump(n=1, force=False):
        if moe_done[0]:
            return
        if not force:
            bctr[0] += 1
            if bctr[0] < b_delay or bctr[0] % b_every != 0:
                return
        for _ in range(n):
            try:
                next(moe_gen)
            except StopIteration:
                moe_done[0] = True
                return

    def pump_a():
        if moe_done[0]:
            return
        actr[0] += 1
        if actr[0] < a_delay or actr[0] % a_every != 0:
            return
        try:
            next(moe_gen)
        except StopIteration:
            moe_done[0] = True

    def proj_feat(dst, dbuf, wname, hc, sl):
        for ttt in range(T // PJW):
            pps = ttt % 2
            last = None
            for c in range(CCH):
                w_ws = [xnb_buf.w, wts_buf[wname][sl].w]
                if c == 0:
                    w_ws += ps_pj_buf[pps].write_waits()
                last = sch.add("tensor", lambda eng, c=c, hc=hc, ttt=ttt,
                               wname=wname, pps=pps, sl=sl: eng.matmul(
                                   pj(pps),
                                   wts[wname][sl][:, c, :],
                                   xnb_sb[:, c, ttt * PJW:(ttt + 1) * PJW],
                                   start=(c == 0), stop=(c == CCH - 1)),
                               waits=w_ws,
                               inc="pe" if c == CCH - 1 else None)
            ps_pj_buf[pps].wrote(last)
            xnb_buf.read(last)
            wts_buf[wname][sl].read(last)
            w_ws = [last] + (dbuf.write_waits() if ttt == 0 else [])
            tcp = sch.add("vector", lambda eng, ttt=ttt, pps=pps,
                          dst=dst: eng.tensor_copy(
                              dst[:, ttt * PJW:(ttt + 1) * PJW],
                              pj(pps)), waits=w_ws, inc="dve")
            ps_pj_buf[pps].read(tcp)
            if ttt == 0:
                dbuf.wrote(tcp)
            else:
                dbuf.w = tcp

    for b in range(B):
        dtok = sch.dma("sync", "dxn", xnb_sb[:],
                       xn_feat_d[:, :, b * T:(b + 1) * T].rearrange(
                           "c p t -> p c t"),
                       waits=xnb_buf.write_waits())
        xnb_buf.wrote(dtok)
        tmo = sch.add("vector", lambda eng: eng.memset(
            v_sb[:, :, :].rearrange("p t (a c) -> p t a c", a=2)
            [:, :, :, 64:65], 1.0),
            waits=qkv_buf["v"].write_waits(), inc="dve")
        qkv_buf["v"].wrote(tmo)

        def load_wts(hc):
            sl = hc % 2
            for n in ("q", "k", "kp", "v"):
                ldw = sch.dma("sync", f"dwt{sl}", wts[n][sl][:],
                              wt_d[n][:, :, hc * 128:(hc + 1) * 128],
                              waits=wts_buf[n][sl].write_waits() +
                              ([("dst", 0)] if b == 0 and hc < 2 else []))
                wts_buf[n][sl].wrote(ldw)

        if b == 0:
            load_wts(0)
        for hc in range(HC):
            sl = hc % 2
            if hc + 1 < HC:
                load_wts(hc + 1)
            elif b == 0:
                load_wts(0)
            proj_feat(q_sb, qkv_buf["q"], "q", hc, sl)
            proj_feat(k_sb, qkv_buf["k"], "k", hc, sl)
            proj_feat(kp_sb, qkv_buf["kp"], "kp", hc, sl)
            for tc_i in range(KC):
                pps = tc_i % 2
                last = None
                for c in range(CCH):
                    w_ws = [xnb_buf.w, wts_buf["v"][sl].w]
                    if c == 0:
                        w_ws += ps_pj_buf[pps].write_waits()
                    last = sch.add("tensor", lambda eng, c=c, hc=hc,
                                   tc_i=tc_i, pps=pps, sl=sl: eng.matmul(
                                       ps_pj[pps][:, 0:128],
                                       xnb_sb[:, c,
                                              tc_i * 128:(tc_i + 1) * 128],
                                       wts["v"][sl][:, c, :],
                                       start=(c == 0), stop=(c == CCH - 1)),
                                   waits=w_ws,
                                   inc="pe" if c == CCH - 1 else None)
                ps_pj_buf[pps].wrote(last)
                xnb_buf.read(last)
                wts_buf["v"][sl].read(last)
                w_ws = [last, qkv_buf["v"].w]
                if tc_i == 0:
                    w_ws += qkv_buf["v"].readers
                tv = sch.add("vector", lambda eng, tc_i=tc_i, pps=pps:
                             eng.tensor_copy(
                                 v_sb[:, tc_i, :].rearrange(
                                     "p (a c) -> p a c", a=2)[:, :, 0:64],
                                 ps_pj[pps][:, 0:128].rearrange(
                                     "p (a c) -> p a c", a=2)),
                             waits=w_ws, inc="dve")
                ps_pj_buf[pps].read(tv)
                if tc_i == 0:
                    qkv_buf["v"].readers = []
                qkv_buf["v"].w = tv

            for qt in range(NQT):
                osl = (hc * NQT + qt) % 2
                for mat in range(2):
                    ksrc, kn = (k_sb, "k") if mat == 0 else (kp_sb, "kp")

                    def emit_scores(kc, ksrc=None, kn=None, qt=qt):
                        par = kc % 2
                        smm = None
                        for h2 in range(2):
                            w_ws = [qkv_buf["q"].w, qkv_buf[kn].w]
                            if h2 == 0:
                                w_ws += sc2_buf[par].write_waits()
                            smm = sch.add("tensor", lambda eng, kc=kc,
                                          qt=qt, ksrc=ksrc, h2=h2, par=par:
                                          eng.matmul(
                                              sc2[par][:, h2, :],
                                              ksrc[h2 * 64:(h2 + 1) * 64,
                                                   kc * 128:(kc + 1) * 128],
                                              q_sb[h2 * 64:(h2 + 1) * 64,
                                                   qt * QT:(qt + 1) * QT],
                                              start=True, stop=True,
                                              tile_position=(h2 * 64, 0)),
                                          waits=w_ws, inc="pe")
                            qkv_buf["q"].read(smm)
                            qkv_buf[kn].read(smm)
                        sc2_buf[par].wrote(smm)
                        return smm

                    smm_next = emit_scores(0, ksrc=ksrc, kn=kn)
                    for kc in range(KC):
                        par = kc % 2
                        smm = smm_next
                        if kc + 1 < KC:
                            smm_next = emit_scores(kc + 1, ksrc=ksrc, kn=kn)
                        tex = sch.add("scalar", lambda eng, par=par:
                                      eng.activation(
                                          aT[par][:], sc2[par][:], AF.Exp,
                                          scale=0.125),
                                      waits=[smm] +
                                      aT_buf[par].write_waits(), inc="act")
                        aT_buf[par].wrote(tex)
                        sc2_buf[par].read(tex)
                        for h2 in range(2):
                            avw = [tex, qkv_buf["v"].w]
                            if kc == 0:
                                avw += ps_o_buf[h2].write_waits()
                            tav = sch.add("tensor", lambda eng, kc=kc,
                                          h2=h2, par=par: eng.matmul(
                                              ps_o[h2][:],
                                              v_sb[:, kc,
                                                   h2 * 65:(h2 + 1) * 65],
                                              aT[par][:, h2, :],
                                              start=(kc == 0),
                                              stop=(kc == KC - 1)),
                                          waits=avw,
                                          inc="pe" if kc == KC - 1
                                          else None)
                            if kc == KC - 1:
                                ps_o_buf[h2].wrote(tav)
                            if tav is not None:
                                aT_buf[par].read(tav)
                                qkv_buf["v"].read(tav)
                        if kc == KC - 1:
                            for h2 in range(2):
                                aT_buf[par].read(ps_o_buf[h2].w)
                                qkv_buf["v"].read(ps_o_buf[h2].w)
                        pump(1)
                    pump_a()
                    for h2 in range(2):
                        tln = sch.add("scalar", lambda eng, h2=h2:
                                      eng.activation(
                                          rc_sb[:], ps_o[h2][64:65, :],
                                          AF.Ln),
                                      waits=[ps_o_buf[h2].w] +
                                      rc_buf.write_waits(), inc="act")
                        rc_buf.wrote(tln)
                        ps_o_buf[h2].read(tln)
                        tnr = sch.add("scalar", lambda eng:
                                      eng.activation(
                                          nrow_bf[:], rc_sb[:], AF.Exp,
                                          scale=-1.0),
                                      waits=[(tln[0], tln[1], True)] +
                                      nrow_buf.write_waits(), inc="act")
                        nrow_buf.wrote(tnr)
                        rc_buf.read(tnr)
                        tbc = sch.add("tensor", lambda eng, mat=mat:
                                      eng.matmul(
                                          ps_pj[0][0:64, 0:QT],
                                          alpharow[0:1, mat, :],
                                          nrow_bf[:], start=True, stop=True),
                                      waits=[tnr, alpharow_buf.w] +
                                      ps_pj_buf[0].write_waits(),
                                      inc="pe")
                        ps_pj_buf[0].wrote(tbc)
                        nrow_buf.read(tbc)
                        tev = sch.add("vector", lambda eng, h2=h2:
                                      eng.tensor_copy(
                                          nbc_sb[h2 * 64:(h2 + 1) * 64, :],
                                          ps_pj[0][0:64, 0:QT]),
                                      waits=[tbc] + nbc_buf.write_waits(),
                                      inc="dve")
                        nbc_buf.wrote(tev)
                        ps_pj_buf[0].read(tev)
                        if mat == 0:
                            tcb = sch.add("vector", lambda eng, h2=h2,
                                          osl=osl: eng.tensor_tensor(
                                              o_stage[osl][
                                                  h2 * 64:(h2 + 1) * 64, :],
                                              ps_o[h2][0:64, :],
                                              nbc_sb[h2 * 64:
                                                     (h2 + 1) * 64, :],
                                              ALU.mult),
                                          waits=[tev, ps_o_buf[h2].w] +
                                          (o_stage_buf[osl].write_waits()
                                           if h2 == 0 else []), inc="dve")
                            if h2 == 0:
                                o_stage_buf[osl].wrote(tcb)
                            else:
                                o_stage_buf[osl].w = tcb
                            ps_o_buf[h2].read(tcb)
                            nbc_buf.read(tcb)
                        else:
                            tcb = sch.add("vector", lambda eng, h2=h2:
                                          eng.tensor_tensor(
                                              otmp_sb[h2 * 64:
                                                      (h2 + 1) * 64, :],
                                              ps_o[h2][0:64, :],
                                              nbc_sb[h2 * 64:
                                                     (h2 + 1) * 64, :],
                                              ALU.mult),
                                          waits=[tev, ps_o_buf[h2].w] +
                                          otmp_buf.write_waits(),
                                          inc="dve")
                            otmp_buf.wrote(tcb)
                            ps_o_buf[h2].read(tcb)
                            nbc_buf.read(tcb)
                            tad = sch.add("vector", lambda eng, h2=h2,
                                          osl=osl: eng.tensor_tensor(
                                              o_stage[osl][
                                                  h2 * 64:(h2 + 1) * 64, :],
                                              o_stage[osl][
                                                  h2 * 64:(h2 + 1) * 64, :],
                                              otmp_sb[h2 * 64:
                                                      (h2 + 1) * 64, :],
                                              ALU.add),
                                          waits=[tcb, o_stage_buf[osl].w],
                                          inc="dve")
                            o_stage_buf[osl].w = tad
                            otmp_buf.read(tad)
                sto = sch.dma("sync", "dst", o_d[:, hc, qt * QT:(qt + 1) * QT],
                              o_stage[osl][:], waits=[o_stage_buf[osl].w])
                o_stage_buf[osl].read(sto)
                pump_a()

        for iN in range(C // PJW):
            ldo = sch.dma("sync", "dwto", wto_h[:],
                          wt_d["o"][:, :, iN * PJW:(iN + 1) * PJW],
                          waits=wto_buf.write_waits())
            wto_buf.wrote(ldo)
            for ts_i in range(T // 128):
                ssl = ts_i % 2
                dso = sch.dma("sync", "dxn", osl_sb[ssl][:],
                              o_d[:, :, ts_i * 128:(ts_i + 1) * 128],
                              waits=osl_buf[ssl].write_waits() +
                              [("dst", 0)])
                osl_buf[ssl].wrote(dso)
                pps = 0
                last = None
                for c in range(HC):
                    w_ws = [osl_buf[ssl].w, wto_buf.w, ("dxn", 0)]
                    if c == 0:
                        w_ws += ps_pj_buf[pps].write_waits()
                    last = sch.add("tensor", lambda eng, ts_i=ts_i, iN=iN,
                                   c=c, pps=pps, ssl=ssl: eng.matmul(
                                       pj(pps),
                                       osl_sb[ssl][:, c, :],
                                       wto_h[:, c, :],
                                       start=(c == 0), stop=(c == HC - 1)),
                                   waits=w_ws,
                                   inc="pe" if c == HC - 1 else None)
                ps_pj_buf[pps].wrote(last)
                osl_buf[ssl].read(last)
                wto_buf.read(last)
                w_ws = [last] + s_stage_buf.write_waits()
                tse = sch.add("vector", lambda eng, iN=iN, pps=pps:
                              eng.tensor_copy(
                                  s_stage[:, iN * PJW:(iN + 1) * PJW],
                                  pj(pps)), waits=w_ws, inc="dve")
                ps_pj_buf[pps].read(tse)
                s_stage_buf.wrote(tse)
                dstore = sch.dma("sync", "dst",
                                 s_tok_d[b * T + ts_i * 128:
                                         b * T + (ts_i + 1) * 128,
                                         iN * PJW:(iN + 1) * PJW],
                                 s_stage[:, iN * PJW:(iN + 1) * PJW],
                                 waits=[s_stage_buf.w])
                s_stage_buf.read(dstore)
                pump_a()

    while not moe_done[0]:
        pump(1, force=True)

    yield "B", sch, es0, esB, None
    sch.barrier()

    esD = contextlib.ExitStack()
    NDS = 4
    f_x = [SB(esD, f"f_x{s}", [128, C], F32) for s in range(NDS)]
    f_g = [SB(esD, f"f_g{s}", [128, C], F32) for s in range(NDS)]
    f_s = [SB(esD, f"f_s{s}", [128, C], F32) for s in range(NDS)]
    f_o = [SB(esD, f"f_o{s}", [128, C], F32) for s in range(NDS)]
    f_bufs = {n: [Buf() for _ in range(NDS)] for n in "xgso"}
    for tt in range(NTT):
        sl = tt % NDS
        dx_ = sch.dma("sync", f"dx{sl}", f_x[sl][:],
                      x_d[tt * 128:(tt + 1) * 128, :],
                      waits=f_bufs["x"][sl].write_waits())
        f_bufs["x"][sl].wrote(dx_)
        dg_ = sch.add("gpsimd", lambda eng, sl=sl, tt=tt:
                      eng.indirect_dma_start(
                          out=f_g[sl][:], out_offset=None,
                          in_=g_slot_d[:, :],
                          in_offset=bass.IndirectOffsetOnAxis(
                              ap=rank_u[:, tt:tt + 1], axis=0)),
                      waits=[rank_u_buf.w] + f_bufs["g"][sl].write_waits(),
                      inc=f"dg{sl}", amt=16)
        f_bufs["g"][sl].wrote(dg_)
        ds_ = sch.dma("sync", f"dx{sl}", f_s[sl][:],
                      s_tok_d[tt * 128:(tt + 1) * 128, :],
                      waits=f_bufs["s"][sl].write_waits())
        f_bufs["s"][sl].wrote(ds_)
        tmu = sch.add("vector", lambda eng, sl=sl: eng.tensor_tensor(
            f_g[sl][:], f_g[sl][:], f_s[sl][:], ALU.mult),
            waits=[dg_, (f"dg{sl}", 0), (f"dx{sl}", 0)], inc="dve")
        f_bufs["g"][sl].w = tmu
        f_bufs["s"][sl].read(tmu)
        tad = sch.add("vector", lambda eng, sl=sl: eng.tensor_tensor(
            f_o[sl][:], f_x[sl][:], f_g[sl][:], ALU.add),
            waits=[tmu, (f"dx{sl}", 0)] + f_bufs["o"][sl].write_waits(),
            inc="dve")
        f_bufs["o"][sl].wrote(tad)
        f_bufs["x"][sl].read(tad)
        f_bufs["g"][sl].read(tad)
        do_ = sch.dma("sync", "dst", out_d[tt * 128:(tt + 1) * 128, :],
                      f_o[sl][:], waits=[tad])
        f_bufs["o"][sl].read(do_)

    sch.add("sync", lambda eng: eng.nop(),
            waits=[("dst", 0), ("dsc", 0), ("dx0", 0), ("dx1", 0),
                   ("dx2", 0), ("dx3", 0), ("dg0", 0), ("dg1", 0),
                   ("dg2", 0), ("dg3", 0), ("dxn", 0)])

    yield "D", sch, es0, esD, None


def build(cfg):
    nc = bass.Bass("TRN2", target_bir_lowering=False, debug=False)
    gen = build_program(cfg, nc)
    with contextlib.ExitStack() as outer:
        first = next(gen)
        sems = {s: outer.enter_context(nc.semaphore(s)) for s in ALL_SEMS}
        with nc.Block() as block:
            phase, sch, es0, es_phase, extra = first
            sch.emit(block, sems)
            es_phase.close()
            if extra is not None:
                extra.close()
            for phase, sch, es0, es_phase, extra in gen:
                sch.emit(block, sems)
                es_phase.close()
                if extra is not None:
                    extra.close()
            es0.close()
    return nc


def kernel(L0_attn_x, L0_attn_W_Q, L0_attn_W_K, L0_attn_W_V,
           L0_attn_W_K_prime, L0_attn_W_O, L0_attn_alpha,
           L0_ffn_wg_red, L0_ffn_wg, L0_ffn_w1, L0_ffn_b1, L0_ffn_w2,
           L0_ffn_b2):
    x = np.ascontiguousarray(np.asarray(L0_attn_x, dtype=np.float32))
    Bfull, T, C = x.shape
    per = Bfull // N_CORES
    shared = {
        "wq": np.ascontiguousarray(np.asarray(L0_attn_W_Q, dtype=np.float32)),
        "wk": np.ascontiguousarray(np.asarray(L0_attn_W_K, dtype=np.float32)),
        "wkp": np.ascontiguousarray(
            np.asarray(L0_attn_W_K_prime, dtype=np.float32)),
        "wv": np.ascontiguousarray(np.asarray(L0_attn_W_V, dtype=np.float32)),
        "wo": np.ascontiguousarray(np.asarray(L0_attn_W_O, dtype=np.float32)),
        "alpha": np.asarray(L0_attn_alpha, dtype=np.float32).reshape(1, 1),
        "wg_red": np.ascontiguousarray(
            np.asarray(L0_ffn_wg_red, dtype=np.float32)),
        "wg": np.ascontiguousarray(np.asarray(L0_ffn_wg, dtype=np.float32)),
        "w1": np.ascontiguousarray(np.asarray(L0_ffn_w1, dtype=np.float32)),
        "b1": np.ascontiguousarray(np.asarray(L0_ffn_b1, dtype=np.float32)),
        "w2": np.ascontiguousarray(np.asarray(L0_ffn_w2, dtype=np.float32)),
        "b2": np.ascontiguousarray(np.asarray(L0_ffn_b2, dtype=np.float32)),
    }
    in_maps = []
    for c in range(N_CORES):
        m = dict(shared)
        m["x"] = np.ascontiguousarray(
            x[c * per:(c + 1) * per].reshape(per * T, C))
        in_maps.append(m)
    nc = build(FULL_CFG)
    res = run_bass_kernel_spmd(nc, in_maps, core_ids=list(range(N_CORES)))
    outs = [np.asarray(r["out"]).reshape(per, T, C) for r in res.results]
    return np.concatenate(outs, axis=0)
